# version 05
# baseline (speedup 1.0000x reference)
import numpy as np
import concourse.bass as bass
import concourse.mybir as mybir
from concourse.bass_utils import run_bass_kernel_spmd

F32 = mybir.dt.float32
I32 = mybir.dt.int32
BF16 = mybir.dt.bfloat16
AF = mybir.ActivationFunctionType
ALU = mybir.AluOpType
AX = mybir.AxisListType

T = 2048
D = 1024
E = 3072
II = 2816
DT = D // 128
IT = II // 128
MAGIC = float(np.float32(2.0 ** 23))
H499 = 0.4999847412109375
CLIP_HI = 255.49
EPS_LN = 1e-5

GROUPS = [[8 + 2 * g, 9 + 2 * g] for g in range(4)] + \
         [[16 + 2 * g, 17 + 2 * g] for g in range(4)] + \
         [[0 + 2 * g, 1 + 2 * g] for g in range(4)]
NG = len(GROUPS)

C_RG, C_RU, C_RD = 0, 22, 44
C_XM, C_XMS, C_AR1 = 52, 53, 54
C_HM, C_HMS, C_AR2 = 55, 56, 57
C_SXB, C_SX2B, C_IV2B, C_IVB = 58, 59, 60, 61
C_IG, C_IU, C_ID = 64, 86, 108

S_A, S_B = 0, T
RW = 2 * T + 16
K_TR = 0
K_GMAX, K_SGC, K_SX0, K_SX, K_INV = 2*T+0, 2*T+1, 2*T+2, 2*T+3, 2*T+4
K_GMAX2, K_SGC2, K_SX20, K_SX2, K_INV2 = 2*T+5, 2*T+6, 2*T+7, 2*T+8, 2*T+9


def build_nc():
    nc = bass.Bass("TRN2", target_bir_lowering=False)

    xT = nc.dram_tensor("xT", [D, T], F32, kind="ExternalInput").ap()
    wqT = nc.dram_tensor("wqT", [D, E], F32, kind="ExternalInput").ap()
    wgT = nc.dram_tensor("wgT", [D, II], F32, kind="ExternalInput").ap()
    wuT = nc.dram_tensor("wuT", [D, II], F32, kind="ExternalInput").ap()
    wdT = nc.dram_tensor("wdT", [II, D], F32, kind="ExternalInput").ap()
    wg_nat = nc.dram_tensor("wg_nat", [II, D], F32, kind="ExternalInput").ap()
    wu_nat = nc.dram_tensor("wu_nat", [II, D], F32, kind="ExternalInput").ap()
    wd_nat = nc.dram_tensor("wd_nat", [D, II], F32, kind="ExternalInput").ap()
    gammaT = nc.dram_tensor("gammaT", [128, DT], F32, kind="ExternalInput").ap()
    clips = nc.dram_tensor("clips", [128, 2], F32, kind="ExternalInput").ap()
    ident = nc.dram_tensor("ident", [128, 128], F32, kind="ExternalInput").ap()
    onesP = nc.dram_tensor("onesP", [128, 1], F32, kind="ExternalInput").ap()
    ones1 = nc.dram_tensor("ones1", [1, 128], F32, kind="ExternalInput").ap()
    outT = nc.dram_tensor("outT", [D, T], F32, kind="ExternalOutput").ap()

    h_spill = nc.dram_tensor("h_spill", [II, T], F32, kind="Internal").ap()
    hq_d = nc.dram_tensor("hq_d", [II, T], BF16, kind="Internal").ap()
    wg_int_d = nc.dram_tensor("wg_int_d", [DT, 128, II], BF16, kind="Internal").ap()
    wu_int_d = nc.dram_tensor("wu_int_d", [DT, 128, II], BF16, kind="Internal").ap()
    wd_int_d = nc.dram_tensor("wd_int_d", [IT, 128, D], BF16, kind="Internal").ap()
    cc1_in = nc.dram_tensor("cc1_in", [128, 1], F32, kind="Internal").ap()
    cc1_out = nc.dram_tensor("cc1_out", [128, 1], F32, kind="Internal",
                             addr_space="Shared").ap()
    cc2_in = nc.dram_tensor("cc2_in", [128, 1], F32, kind="Internal").ap()
    cc2_out = nc.dram_tensor("cc2_out", [128, 1], F32, kind="Internal",
                             addr_space="Shared").ap()

    sem_names = [
        "s_cld", "s_x", "s_sq", "s_stm", "s_sev", "s_rows", "s_tr", "s_repe",
        "s_tc", "s_xh", "s_xl", "s_nat", "s_wst", "s_wq", "s_ah", "s_al",
        "s_mm", "s_ev", "s_sig", "s_scan", "s_res", "s_xm", "s_cc1", "s_inv",
        "s_xq", "s_wg", "s_qnt", "s_ir", "s_ire", "s_ws", "s_sg", "s_gmm",
        "s_gev", "s_gsl", "s_hsp", "s_cc2", "s_inv2", "s_wd", "s_dq", "s_wds",
        "s_sdb", "s_hrb", "s_hqs", "s_hq", "s_dmm", "s_dev", "s_out",
        "s_hqm", "s_hqd", "s_sqr", "s_nat2", "s_sg2", "s_hsp2", "s_hqm2", "s_out2", "s_cc1b", "s_cc2b", "s_wq2", "s_ivd", "s_ivr", "s_sgd", "s_sud", "s_sgq", "s_suq", "s_wsg", "s_wsu",
    ]
    S = {n: nc.semaphore(n).__enter__() for n in sem_names}

    def sb(name, shape, dt):
        g = nc.sbuf_tensor(name, shape, dt)
        return g, g.__enter__()

    def psu(name, shape):
        g = nc.psum_tensor(name, shape, F32)
        return g, g.__enter__()

    _, rows = sb("rows", [128, RW], F32)
    _, acc = sb("acc", [128, DT * T], F32)
    _, rep1 = sb("rep1", [128, T], F32)
    _, repm = sb("repm", [128, T], F32)
    _, rep3 = sb("rep3", [128, T], F32)
    _, gam = sb("gam", [128, DT], F32)
    _, idn = sb("idn", [128, 128], F32)
    _, onp = sb("onp", [128, 1], F32)
    _, on1 = sb("on1", [1, 128], F32)
    _, clp = sb("clp", [128, 2], F32)
    _, scal = sb("scal", [128, 128], F32)
    _, b128 = sb("b128", [128, 1], F32)

    g_xh, xh = sb("xh", [128, DT * T], BF16)
    g_xl, xl = sb("xl", [128, DT * T], BF16)

    gA = []
    def sbA(name, shape, dt):
        g, t = sb(name, shape, dt)
        gA.append(g)
        return t
    xstage = sbA("xstage", [128, T], F32)
    wqst = sbA("wqst", [128, 2 * 256], F32)
    ahg = sbA("ahg", [128, 2 * DT * 256], BF16)
    alg = sbA("alg", [128, 2 * DT * 256], BF16)
    rot = sbA("rot", [128, T], F32)
    evt = sbA("evt", [128, 1024], F32)

    g_pt, ps_t = psu("ps_t", [128, 128])
    g_pb, ps_b = psu("ps_b", [128, 512])
    g_pst, ps_st = psu("ps_st", [1, 1024])
    g_pst2, ps_st2 = psu("ps_st2", [1, 1024])

    blkg = nc.Block()
    block = blkg.__enter__()

    def stsl(buf, g, j, ei):
        base = (g % 2) * (DT * 256) + j * 256 + ei * 128
        return buf[:, base:base + 128]


    @block.sync
    def _(sy):
        for dst, src in ((gam, gammaT), (idn, ident), (onp, onesP),
                         (on1, ones1), (clp, clips)):
            sy.dma_start(out=dst[:, :], in_=src).then_inc(S["s_cld"], 16)
        for j in range(DT):
            if j >= 1:
                sy.wait_ge(S["s_sq"], 4 * j)
                sy.wait_ge(S["s_stm"], 8 * j)
            sy.dma_start(out=xstage[:, :],
                         in_=xT[128 * j:128 * (j + 1), :]).then_inc(S["s_x"], 16)
        for j in range(DT):
            if j == 0:
                sy.wait_ge(S["s_sq"], 4 * DT)
                sy.wait_ge(S["s_stm"], 8 * DT)
            else:
                sy.wait_ge(S["s_tc"], j)
            sy.dma_start(out=xstage[:, :],
                         in_=xT[128 * j:128 * (j + 1), :]).then_inc(S["s_x"], 16)
        for g in range(NG):
            e0 = GROUPS[g][0] * 128
            for j in range(DT):
                k = g * DT + j
                if k >= 2:
                    sy.wait_ge(S["s_al"], k - 1)
                sl = slice(256 * (k % 2), 256 * (k % 2) + 256)
                sy.dma_start(out=wqst[:, sl],
                             in_=wqT[128 * j:128 * (j + 1), e0:e0 + 256]
                             ).then_inc(S["s_wq"] if k % 2 == 0 else S["s_wq2"], 16)

    @block.scalar
    def _(sc):
        for j in range(DT):
            sc.wait_ge(S["s_x"], 16 * (j + 1))
            for c in range(4):
                idx = 4 * j + c
                if idx >= 2:
                    pj, pc = (idx - 2) // 4, (idx - 2) % 4
                    sc.wait_ge(S["s_stm"], 8 * pj + 4 + pc + 1)
                sc.activation(out=evt[:, 512 * (c % 2):512 * (c % 2) + 512],
                              in_=xstage[:, 512 * c:512 * (c + 1)],
                              func=AF.Square).then_inc(S["s_sq"], 1)

    @block.tensor
    def _(tn):
        tn.wait_ge(S["s_cld"], 80)
        for j in range(DT):
            tn.wait_ge(S["s_x"], 16 * (j + 1))
            for c in range(4):
                idx = 4 * j + c
                if idx >= 2:
                    pj, pc = (idx - 2) // 4, (idx - 2) % 4
                    tn.wait_ge(S["s_sev"], 8 * pj + pc + 1)
                tn.matmul(ps_st[0:1, 512 * (c % 2):512 * (c % 2) + 512],
                          onp[:, 0:1], xstage[:, 512 * c:512 * (c + 1)],
                          start=True, stop=True).then_inc(S["s_stm"], 1)
            for c in range(4):
                tn.wait_ge(S["s_sq"], 4 * j + c + 1)
                idx = 4 * j + c
                if idx >= 2:
                    pj, pc = (idx - 2) // 4, (idx - 2) % 4
                    tn.wait_ge(S["s_sev"], 8 * pj + 4 + pc + 1)
                tn.matmul(ps_st2[0:1, 512 * (c % 2):512 * (c % 2) + 512],
                          onp[:, 0:1], evt[:, 512 * (c % 2):512 * (c % 2) + 512],
                          start=True, stop=True).then_inc(S["s_stm"], 1)

    @block.vector
    def _(v):
        v.memset(b128[:, :], 128.0)
        for j in range(DT):
            for c in range(4):
                v.wait_ge(S["s_stm"], 8 * j + c + 1)
                src = ps_st[0:1, 512 * (c % 2):512 * (c % 2) + 512]
                dst = rows[0:1, S_A + 512 * c:S_A + 512 * (c + 1)]
                if j > 0:
                    v.drain()
                op = v.tensor_copy(out=dst, in_=src) if j == 0 else \
                    v.tensor_add(out=dst, in0=dst, in1=src)
                op.then_inc(S["s_sev"], 1)
            for c in range(4):
                v.wait_ge(S["s_stm"], 8 * j + 4 + c + 1)
                src = ps_st2[0:1, 512 * (c % 2):512 * (c % 2) + 512]
                dst = rows[0:1, S_B + 512 * c:S_B + 512 * (c + 1)]
                if j > 0:
                    v.drain()
                op = v.tensor_copy(out=dst, in_=src) if j == 0 else \
                    v.tensor_add(out=dst, in0=dst, in1=src)
                op.then_inc(S["s_sev"], 1)
        rA = rows[0:1, S_A:S_A + T]
        rB = rows[0:1, S_B:S_B + T]
        v.drain()
        v.tensor_scalar(out=rA, in0=rA, scalar1=1.0 / D, scalar2=None,
                        op0=ALU.mult).then_inc(S["s_rows"], 1)
        for c in range(4):
            v.wait_ge(S["s_tr"], c + 1)
            v.tensor_copy(out=repm[:, 512 * c:512 * (c + 1)],
                          in_=ps_b[:, :]).then_inc(S["s_repe"], 1)
        v.drain()
        v.tensor_scalar(out=rB, in0=rB, scalar1=1.0 / D, scalar2=None,
                        op0=ALU.mult).then_inc(S["s_rows"], 1)
        v.drain()
        v.tensor_mul(out=rA, in0=rA, in1=rA).then_inc(S["s_rows"], 1)
        v.drain()
        v.tensor_sub(out=rB, in0=rB, in1=rA).then_inc(S["s_rows"], 1)
        v.drain()
        v.reciprocal(out=rA, in_=rB).then_inc(S["s_rows"], 1)
        for c in range(4):
            v.wait_ge(S["s_tr"], 5 + c)
            v.tensor_copy(out=rep1[:, 512 * c:512 * (c + 1)],
                          in_=ps_b[:, :]).then_inc(S["s_repe"], 1)
        v.drain()
        v.tensor_scalar(out=rB, in0=rB, scalar1=EPS_LN, scalar2=None,
                        op0=ALU.add).then_inc(S["s_rows"], 1)
        v.wait_ge(S["s_tr"], 8)
        v.drain()
        v.reciprocal(out=rA, in_=rB).then_inc(S["s_rows"], 1)
        for c in range(4):
            v.wait_ge(S["s_tr"], 9 + c)
            v.tensor_copy(out=rep3[:, 512 * c:512 * (c + 1)],
                          in_=ps_b[:, :]).then_inc(S["s_repe"], 1)

    @block.scalar
    def _(sc):
        sc.wait_ge(S["s_repe"], 8)
        sc.activation(out=rep1[:, :], in_=rep1[:, :],
                      func=AF.Sqrt).then_inc(S["s_sqr"], 1)
        sc.wait_ge(S["s_repe"], 12)
        sc.activation(out=rep3[:, :], in_=rep3[:, :],
                      func=AF.Sqrt).then_inc(S["s_sqr"], 1)

    g_pst2.__exit__(None, None, None)
    g_pst.__exit__(None, None, None)
    g_pq0, ps_q0 = psu("ps_q0", [128, 1024])
    g_pq1, ps_q1 = psu("ps_q1", [128, 1024])
    ps_qq = [ps_q0, ps_q1]

    @block.tensor
    def _(tn):
        nb = 0
        for src_off, rowwait in ((S_A, 1), (S_A, 5), (S_A, 7)):
            tn.wait_ge(S["s_rows"], rowwait)
            for c in range(4):
                if nb >= 1:
                    tn.wait_ge(S["s_repe"], nb)
                tn.matmul(ps_b[:, :], on1[0:1, :],
                          rows[0:1, src_off + 512 * c:src_off + 512 * (c + 1)],
                          start=True, stop=True).then_inc(S["s_tr"], 1)
                nb += 1

    @block.vector
    def _(v):
        v.wait_ge(S["s_repe"], 8)
        for j in range(DT):
            v.wait_ge(S["s_x"], 16 * (DT + j + 1))
            if j >= 1:
                v.wait_ge(S["s_xh"], j)
            v.drain()
            v.scalar_tensor_tensor(out=rot[:, :], in0=repm[:, :], scalar=-1.0,
                                   in1=xstage[:, :], op0=ALU.mult,
                                   op1=ALU.add).then_inc(S["s_tc"], 1)
            v.wait_ge(S["s_xh"], j + 1)
            v.tensor_sub(out=xl[:, T * j:T * (j + 1)], in0=rot[:, :],
                         in1=xh[:, T * j:T * (j + 1)]).then_inc(S["s_xl"], 1)

    @block.scalar
    def _(sc):
        for j in range(DT):
            sc.wait_ge(S["s_tc"], j + 1)
            sc.copy(out=xh[:, T * j:T * (j + 1)],
                    in_=rot[:, :]).then_inc(S["s_xh"], 1)

    @block.vector
    def _(v):
        v.wait_ge(S["s_xl"], DT)
        v.wait_ge(S["s_sqr"], 2)
        v.wait_ge(S["s_stm"], 8 * DT)
        for j in range(DT):
            for h2 in range(2):
                sl = slice(T * j + 1024 * h2, T * j + 1024 * (h2 + 1))
                v.drain()
                v.tensor_add(out=evt[:, 0:1024], in0=xh[:, sl], in1=xl[:, sl])
                v.drain()
                v.tensor_mul(out=evt[:, 0:1024], in0=evt[:, 0:1024],
                             in1=rep3[:, 1024 * h2:1024 * (h2 + 1)])
                v.drain()
                if j == 0 and h2 == 0:
                    v.reduce_max(out=scal[:, C_XM:C_XM + 1], in_=evt[:, 0:1024],
                                 axis=AX.X, apply_absolute_value=True
                                 ).then_inc(S["s_xm"], 1)
                else:
                    v.reduce_max(out=scal[:, C_XMS:C_XMS + 1], in_=evt[:, 0:1024],
                                 axis=AX.X, apply_absolute_value=True)
                    v.drain()
                    v.tensor_max(out=scal[:, C_XM:C_XM + 1],
                                 in0=scal[:, C_XM:C_XM + 1],
                                 in1=scal[:, C_XMS:C_XMS + 1]).then_inc(S["s_xm"], 1)

    nat = []
    for i in range(IT):
        nat.append((wg_nat, 128 * i, 0, 1024, C_RG + i, False))
    for i in range(IT):
        nat.append((wu_nat, 128 * i, 0, 1024, C_RU + i, False))
    for hh in range(DT):
        for part in range(3):
            w0 = 1024 * part
            wid = 1024 if part < 2 else II - 2048
            nat.append((wd_nat, 128 * hh, w0, wid, C_RD + hh, part > 0))

    @block.gpsimd
    def _(gp):
        gp.wait_ge(S["s_xl"], DT)
        for t, (src, r0, w0, wid, col, foldin) in enumerate(nat):
            if t >= 2:
                gp.wait_ge(S["s_wst"], t - 1)
            buf = xstage[:, 1024 * (t % 2):1024 * (t % 2) + 1024]
            sem = S["s_nat"] if t % 2 == 0 else S["s_nat2"]
            gp.dma_start(out=buf[:, 0:wid],
                         in_=src[r0:r0 + 128, w0:w0 + wid]).then_inc(sem, 16)
        gp.wait_ge(S["s_xm"], 16)
        gp.dma_start(out=cc1_in, in_=scal[:, C_XM:C_XM + 1]).then_inc(S["s_cc1"], 16)
        gp.wait_ge(S["s_cc1"], 16)
        gp.collective_compute("AllReduce", ALU.max,
                              replica_groups=[[0, 1, 2, 3, 4, 5, 6, 7]],
                              ins=[cc1_in], outs=[cc1_out]).then_inc(S["s_cc1b"], 1)
        gp.wait_ge(S["s_cc1b"], 1)
        gp.dma_start(out=scal[:, C_AR1:C_AR1 + 1], in_=cc1_out
                     ).then_inc(S["s_cc1"], 16)

    @block.scalar
    def _(sc):
        def sigmoids(ga):
            for kk in range(4):
                k = (ga - 4) * 4 + kk
                nh = kk // 2
                ei = kk % 2
                b = ga * 4 + nh * 2 + ei
                sc.wait_ge(S["s_ev"], b + 1)
                if k >= 1:
                    sc.wait_ge(S["s_scan"], k)
                sc.activation(out=evt[:, 0:1024],
                              in_=rot[:, 1024 * ei:1024 * (ei + 1)],
                              func=AF.Sigmoid).then_inc(S["s_sig"], 1)

        sc.wait_ge(S["s_cld"], 80)
        for g in range(NG):
            for j in range(DT):
                k = g * DT + j
                sc.wait_ge(S["s_wq"] if k % 2 == 0 else S["s_wq2"],
                           16 * (k // 2 + 1))
                if g >= 2 and j == 0:
                    sc.wait_ge(S["s_mm"], 4 * (g - 1))
                base = (g % 2) * (DT * 256) + j * 256
                sl = slice(256 * (k % 2), 256 * (k % 2) + 256)
                sc.activation(out=ahg[:, base:base + 256], in_=wqst[:, sl],
                              func=AF.Copy, scale=gam[:, j:j + 1]
                              ).then_inc(S["s_ah"], 1)
            if 5 <= g <= 8:
                sigmoids(g - 1)

    @block.vector
    def _(v):
        def prep(g):
            for j in range(DT):
                k = g * DT + j
                v.wait_ge(S["s_ah"], k + 1)
                base = (g % 2) * (DT * 256) + j * 256
                sl = slice(256 * (k % 2), 256 * (k % 2) + 256)
                v.scalar_tensor_tensor(out=alg[:, base:base + 256],
                                       in0=wqst[:, sl],
                                       scalar=gam[:, j:j + 1],
                                       in1=ahg[:, base:base + 256],
                                       op0=ALU.mult,
                                       op1=ALU.subtract).then_inc(S["s_al"], 1)

        def consume(g):
            gkind = "kv" if g < 4 else ("a" if g < 8 else "q")
            v.wait_ge(S["s_sqr"], 1)
            for nh in range(2):
                items = []
                for ei in range(2):
                    b = g * 4 + nh * 2 + ei
                    e = GROUPS[g][ei]
                    pq = ps_qq[b % 2]
                    v.wait_ge(S["s_mm"], b + 1)
                    if gkind == "kv":
                        jj = e - 8
                        dst = acc[:, T * jj + 1024 * nh:T * jj + 1024 * (nh + 1)]
                    else:
                        if gkind == "a":
                            sigidx = (g - 4) * 4 + nh * 2 + ei
                            if sigidx >= 2:
                                v.wait_ge(S["s_sig"], sigidx - 1)
                        elif g == 8 and nh == 0:
                            v.wait_ge(S["s_sig"], 16)
                        dst = rot[:, 1024 * ei:1024 * (ei + 1)]
                    v.tensor_mul(out=dst, in0=pq[:, :],
                                 in1=rep1[:, 1024 * nh:1024 * (nh + 1)]
                                 ).then_inc(S["s_ev"], 1)
                    items.append((b, e, ei))
                if gkind == "a":
                    for (b, e, ei) in items:
                        jj = e - 16
                        sigidx = (g - 4) * 4 + nh * 2 + ei
                        v.wait_ge(S["s_sig"], sigidx + 1)
                        seg = acc[:, T * jj + 1024 * nh:T * jj + 1024 * (nh + 1)]
                        ini = 0.0 if nh == 0 else \
                            acc[:, T * jj + 1024 * nh - 1:T * jj + 1024 * nh]
                        v.drain()
                        v.tensor_tensor_scan(out=seg, data0=evt[:, 0:1024],
                                             data1=seg, initial=ini,
                                             op0=ALU.mult, op1=ALU.add
                                             ).then_inc(S["s_scan"], 1)
                elif gkind == "q":
                    for (b, e, ei) in items:
                        jj = e
                        sl = slice(T * jj + 1024 * nh, T * jj + 1024 * (nh + 1))
                        seg = acc[:, sl]
                        src = rot[:, 1024 * ei:1024 * (ei + 1)]
                        v.drain()
                        v.tensor_mul(out=seg, in0=seg, in1=src)
                        v.drain()
                        v.tensor_add(out=seg, in0=seg, in1=xh[:, sl])
                        v.drain()
                        v.tensor_add(out=seg, in0=seg, in1=xl[:, sl])
                        v.drain()
                        v.tensor_add(out=seg, in0=seg,
                                     in1=repm[:, 1024 * nh:1024 * (nh + 1)]
                                     ).then_inc(S["s_res"], 1)

        nat_t = [0]

        def nat_steps(n):
            while nat_t[0] < len(nat) and n > 0:
                t = nat_t[0]
                (srcw, r0, w0, wid, col, foldin) = nat[t]
                buf = xstage[:, 1024 * (t % 2):1024 * (t % 2) + 1024]
                v.wait_ge(S["s_nat"] if t % 2 == 0 else S["s_nat2"],
                          16 * (t // 2 + 1))
                v.drain()
                if not foldin:
                    v.reduce_max(out=scal[:, col:col + 1], in_=buf[:, 0:wid],
                                 axis=AX.X, apply_absolute_value=True
                                 ).then_inc(S["s_wst"], 1)
                else:
                    v.reduce_max(out=scal[:, C_HMS:C_HMS + 1], in_=buf[:, 0:wid],
                                 axis=AX.X, apply_absolute_value=True)
                    v.drain()
                    v.tensor_max(out=scal[:, col:col + 1],
                                 in0=scal[:, col:col + 1],
                                 in1=scal[:, C_HMS:C_HMS + 1]).then_inc(S["s_wst"], 1)
                nat_t[0] += 1
                n -= 1

        prep(0)
        prep(1)
        for g in range(NG):
            consume(g)
            if g + 2 < NG:
                prep(g + 2)
            nat_steps(6)
        nat_steps(len(nat))


    @block.tensor
    def _(tn):
        tn.wait_ge(S["s_xl"], DT)
        b = 0
        for g in range(NG):
            tn.wait_ge(S["s_al"], DT * (g + 1))
            for nh in range(2):
                for ei in range(2):
                    par = b % 2
                    if b >= 2:
                        tn.wait_ge(S["s_ev"], b - 1)
                    p = ps_qq[par]
                    mm = None
                    for j in range(DT):
                        st_h = stsl(ahg, g, j, ei)
                        st_l = stsl(alg, g, j, ei)
                        c0 = slice(T * j + 1024 * nh, T * j + 1024 * nh + 512)
                        c1 = slice(T * j + 1024 * nh + 512, T * j + 1024 * (nh + 1))
                        first = (j == 0)
                        last = (j == DT - 1)
                        tn.matmul(p[:, 0:512], st_h, xh[:, c0], start=first, stop=False)
                        tn.matmul(p[:, 512:1024], st_h, xh[:, c1], start=first, stop=False)
                        tn.matmul(p[:, 0:512], st_h, xl[:, c0], start=False, stop=False)
                        tn.matmul(p[:, 512:1024], st_h, xl[:, c1], start=False, stop=False)
                        tn.matmul(p[:, 0:512], st_l, xh[:, c0], start=False, stop=last)
                        mm = tn.matmul(p[:, 512:1024], st_l, xh[:, c1],
                                       start=False, stop=last)
                    mm.then_inc(S["s_mm"], 1)
                    b += 1

    @block.tensor
    def _(tn):
        tn.wait_ge(S["s_cc1"], 32)
        tn.transpose(ps_t[0:1, 0:128], scal[:, C_AR1:C_AR1 + 1],
                     idn[:, :]).then_inc(S["s_inv"], 1)
        tn.wait_ge(S["s_inv"], 7)
        tn.wait_ge(S["s_repe"], 12)
        tn.matmul(ps_b[:, 0:1], on1[0:1, :], rows[0:1, K_INV:K_INV + 1],
                  start=True, stop=True).then_inc(S["s_inv"], 1)
        tn.wait_ge(S["s_inv"], 10)
        tn.matmul(ps_b[:, 0:1], on1[0:1, :], rows[0:1, K_SX:K_SX + 1],
                  start=True, stop=True).then_inc(S["s_inv"], 1)

    @block.vector
    def _(v):
        v.wait_ge(S["s_inv"], 1)
        v.drain()
        v.tensor_copy(out=rows[0:1, K_TR:K_TR + 128],
                      in_=ps_t[0:1, 0:128]).then_inc(S["s_inv"], 1)
        v.drain()
        v.reduce_max(out=rows[0:1, K_GMAX:K_GMAX + 1],
                     in_=rows[0:1, K_TR:K_TR + 128],
                     axis=AX.X).then_inc(S["s_inv"], 1)

    @block.scalar
    def _(sc):
        sc.wait_ge(S["s_inv"], 3)
        sc.activation(out=rows[0:1, K_SGC:K_SGC + 1],
                      in_=clp[0:1, 0:1],
                      func=AF.Sigmoid).then_inc(S["s_inv"], 1)

    @block.vector
    def _(v):
        v.wait_ge(S["s_inv"], 4)
        v.drain()
        v.tensor_mul(out=rows[0:1, K_SX0:K_SX0 + 1],
                     in0=rows[0:1, K_GMAX:K_GMAX + 1],
                     in1=rows[0:1, K_SGC:K_SGC + 1]).then_inc(S["s_inv"], 1)
        v.drain()
        v.tensor_scalar(out=rows[0:1, K_SX:K_SX + 1],
                        in0=rows[0:1, K_SX0:K_SX0 + 1],
                        scalar1=1.0 / 127.0, scalar2=None,
                        op0=ALU.mult).then_inc(S["s_inv"], 1)
        v.drain()
        v.reciprocal(out=rows[0:1, K_INV:K_INV + 1],
                     in_=rows[0:1, K_SX:K_SX + 1]).then_inc(S["s_inv"], 1)
        v.wait_ge(S["s_inv"], 8)
        v.drain()
        v.tensor_copy(out=scal[:, C_IVB:C_IVB + 1],
                      in_=ps_b[:, 0:1]).then_inc(S["s_inv"], 1)
        v.drain()
        v.tensor_scalar(out=rep3[:, :], in0=rep3[:, :],
                        scalar1=scal[:, C_IVB:C_IVB + 1], scalar2=None,
                        op0=ALU.mult).then_inc(S["s_inv"], 1)
        v.wait_ge(S["s_inv"], 11)
        v.drain()
        v.tensor_copy(out=scal[:, C_SXB:C_SXB + 1],
                      in_=ps_b[:, 0:1]).then_inc(S["s_inv"], 1)
        v.wait_ge(S["s_wst"], 44)
        v.drain()
        v.reciprocal(out=scal[:, C_IG:C_IG + IT], in_=scal[:, C_RG:C_RG + IT])
        v.drain()
        v.tensor_scalar(out=scal[:, C_IG:C_IG + IT], in0=scal[:, C_IG:C_IG + IT],
                        scalar1=127.0, scalar2=None, op0=ALU.mult)
        v.drain()
        v.reciprocal(out=scal[:, C_IU:C_IU + IT], in_=scal[:, C_RU:C_RU + IT])
        v.drain()
        v.tensor_scalar(out=scal[:, C_IU:C_IU + IT], in0=scal[:, C_IU:C_IU + IT],
                        scalar1=127.0, scalar2=None, op0=ALU.mult)
        v.wait_ge(S["s_wst"], 68)
        v.drain()
        v.reciprocal(out=scal[:, C_ID:C_ID + DT], in_=scal[:, C_RD:C_RD + DT])
        v.drain()
        v.tensor_scalar(out=scal[:, C_ID:C_ID + DT], in0=scal[:, C_ID:C_ID + DT],
                        scalar1=127.0, scalar2=None, op0=ALU.mult)
        v.drain()
        v.tensor_scalar(out=scal[:, C_RG:C_RG + IT], in0=scal[:, C_RG:C_RG + IT],
                        scalar1=scal[:, C_SXB:C_SXB + 1], scalar2=1.0 / 127.0,
                        op0=ALU.mult, op1=ALU.mult).then_inc(S["s_inv"], 1)
        v.drain()
        v.tensor_scalar(out=scal[:, C_RU:C_RU + IT], in0=scal[:, C_RU:C_RU + IT],
                        scalar1=scal[:, C_SXB:C_SXB + 1], scalar2=1.0 / 127.0,
                        op0=ALU.mult, op1=ALU.mult).then_inc(S["s_inv"], 1)

    for g in reversed(gA):
        g.__exit__(None, None, None)
    g_cbf, cbf = sb("cbf", [128, 2048], F32)
    g_cbi, cbi = sb("cbi", [128, 2048], I32)
    xq = xl

    def xq_chain(eng, j, h2, half):
        sl = slice(T * j + 1024 * h2, T * j + 1024 * (h2 + 1))
        cf = cbf[:, 1024 * half:1024 * (half + 1)]
        ci = cbi[:, 1024 * half:1024 * (half + 1)]
        eng.drain()
        eng.tensor_add(out=cf, in0=xh[:, sl], in1=xl[:, sl])
        eng.drain()
        eng.tensor_mul(out=cf, in0=cf, in1=rep3[:, 1024 * h2:1024 * (h2 + 1)])
        eng.drain()
        eng.tensor_scalar(out=cf, in0=cf, scalar1=128.0,
                          scalar2=CLIP_HI, op0=ALU.add, op1=ALU.min)
        eng.drain()
        eng.tensor_scalar(out=cf, in0=cf, scalar1=0.0,
                          scalar2=MAGIC, op0=ALU.max, op1=ALU.add)
        eng.drain()
        eng.tensor_scalar(out=ci, in0=cf, scalar1=MAGIC,
                          scalar2=H499, op0=ALU.subtract, op1=ALU.add)
        eng.drain()
        return eng.tensor_scalar(out=xq[:, sl], in0=ci, scalar1=128,
                                 scalar2=None, op0=ALU.subtract)

    @block.vector
    def _(v):
        v.wait_ge(S["s_mm"], 4 * NG)
        v.wait_ge(S["s_sig"], 16)
        v.wait_ge(S["s_wst"], 68)
        v.wait_ge(S["s_inv"], 12)
        for j in range(DT):
            for h2 in range(2):
                if (2 * j + h2) % 2 == 0:
                    xq_chain(v, j, h2, 0).then_inc(S["s_xq"], 1)

    @block.gpsimd
    def _(gp):
        gp.wait_ge(S["s_mm"], 4 * NG)
        gp.wait_ge(S["s_sig"], 16)
        gp.wait_ge(S["s_wst"], 68)
        gp.wait_ge(S["s_inv"], 12)
        for j in range(DT):
            for h2 in range(2):
                if (2 * j + h2) % 2 == 1:
                    xq_chain(gp, j, h2, 1).then_inc(S["s_xq"], 1)

    g_cbi.__exit__(None, None, None)
    g_cbf.__exit__(None, None, None)

    sgb = xh[:, 0:2048]
    sub = xh[:, 2048:4096]
    wibf = xh[:, 4096:4096 + 1024]
    hqo = xh[:, 5120:5120 + 2048]
    wdst = xh[:, 7168:7168 + 2048]
    hqmv = xh[:, 9216:9216 + 1024]
    gqsp = xh[:, 10240:10240 + II]
    uqsp = xh[:, 13056:13056 + II]

    g_binv, b1inv = sb("b1inv", [128, 2 * II], F32)
    inv_g = b1inv[:, 0:II]
    inv_u = b1inv[:, II:2 * II]
    inv_d = rows[:, 3072:4096]
    g_b1f, b1f = sb("b1f", [128, 1408], F32)
    sgst = b1f[:, 0:1408]
    sust = rows[:, 0:1408]
    b1t = rep3[:, 0:1024]
    b2t = rep3[:, 1024:2048]
    g_b1i, b1i = sb("b1i", [128, 1024], I32)
    g_b2i, b2i = sb("b2i", [128, 1024], I32)
    wdstg = rows[:, S_B:S_B + 1024]
    g1b = rep1[:, 0:1024]
    hb = repm[:, 0:2048]
    irow = rows[0:1, 0:II]
    invrow_d = nc.dram_tensor("invrow_d", [3, II], F32, kind="Internal").ap()

    IV = [(C_IG, IT, inv_g), (C_IU, IT, inv_u), (C_ID, DT, inv_d)]
    IV_NCH = [(nt * 128 + 511) // 512 for (_, nt, _) in IV]
    IV_IR = [0, 1 + IV_NCH[0], 2 + IV_NCH[0] + IV_NCH[1]]
    IV_IRE = [0, 1 + IV_NCH[0], 2 + IV_NCH[0] + IV_NCH[1]]
    IV_IR_TOT = 3 + sum(IV_NCH)
    IV_IRE_TOT = 3 + sum(IV_NCH)

    @block.tensor
    def _(tn):
        tn.wait_ge(S["s_inv"], 14)
        for nb, (col, nt, dst) in enumerate(IV):
            if nb >= 1:
                tn.wait_ge(S["s_ire"], IV_IRE[nb])
            tn.transpose(ps_t[0:nt, 0:128], scal[:, col:col + nt],
                         idn[:, :]).then_inc(S["s_ir"], 1)
            for c in range(IV_NCH[nb]):
                w = min(512, nt * 128 - 512 * c)
                tn.wait_ge(S["s_ivr"], 16 * (nb + 1))
                if c >= 1:
                    tn.wait_ge(S["s_ire"], IV_IRE[nb] + 1 + c)
                tn.matmul(ps_b[:, 0:w], on1[0:1, :],
                          irow[0:1, 512 * c:512 * c + w],
                          start=True, stop=True).then_inc(S["s_ir"], 1)

    @block.vector
    def _(v):
        v.wait_ge(S["s_xq"], 16)
        for nb, (col, nt, dst) in enumerate(IV):
            v.wait_ge(S["s_ir"], IV_IR[nb] + 1)
            if nb >= 1:
                v.wait_ge(S["s_ivd"], 16 * nb)
            v.drain()
            v.tensor_copy(out=b1t[0:nt, 0:128],
                          in_=ps_t[0:nt, 0:128]).then_inc(S["s_ire"], 1)
            for c in range(IV_NCH[nb]):
                w = min(512, nt * 128 - 512 * c)
                v.wait_ge(S["s_ir"], IV_IR[nb] + 1 + c + 1)
                v.drain()
                v.tensor_copy(out=dst[:, 512 * c:512 * c + w],
                              in_=ps_b[:, 0:w]).then_inc(S["s_ire"], 1)

    @block.gpsimd
    def _(gp):
        for nb, (col, nt, dst) in enumerate(IV):
            gp.wait_ge(S["s_ire"], IV_IRE[nb] + 1)
            gp.dma_start(out=invrow_d[nb, 0:nt * 128],
                         in_=b1t[0:nt, 0:128]).then_inc(S["s_ivd"], 16)

    @block.sync
    def _(sy):
        for nb, (col, nt, dst) in enumerate(IV):
            sy.wait_ge(S["s_ivd"], 16 * (nb + 1))
            if nb >= 1:
                sy.wait_ge(S["s_ir"], IV_IR[nb])
            sy.dma_start(out=irow[0:1, 0:nt * 128],
                         in_=invrow_d[nb, 0:nt * 128]).then_inc(S["s_ivr"], 16)

    @block.sync
    def _(sy):
        sy.wait_ge(S["s_xq"], 16)
        sy.wait_ge(S["s_ire"], IV_IRE_TOT)
        for j in range(DT):
            for hf in range(2):
                k = 2 * j + hf
                if k >= 1:
                    sy.wait_ge(S["s_sgq"], k)
                sy.dma_start(out=sgst,
                             in_=wgT[128 * j:128 * (j + 1),
                                     1408 * hf:1408 * (hf + 1)]
                             ).then_inc(S["s_sgd"], 16)
                if k >= 1:
                    sy.wait_ge(S["s_suq"], k)
                sy.dma_start(out=sust,
                             in_=wuT[128 * j:128 * (j + 1),
                                     1408 * hf:1408 * (hf + 1)]
                             ).then_inc(S["s_sud"], 16)
        for i in range(IT):
            if i >= 2:
                sy.wait_ge(S["s_gmm"], 2 * (i - 1))
            sy.wait_ge(S["s_wsg"], 16 * DT)
            sy.wait_ge(S["s_wsu"], 16 * DT)
            sem = S["s_sg"] if i % 2 == 0 else S["s_sg2"]
            sy.dma_start(
                out=sgb[:, 1024 * (i % 2):1024 * (i % 2) + 1024]
                .rearrange("p (j c) -> p j c", c=128),
                in_=wg_int_d[:, :, 128 * i:128 * (i + 1)]
                .rearrange("j p c -> p j c")).then_inc(sem, 16)
            sy.dma_start(
                out=sub[:, 1024 * (i % 2):1024 * (i % 2) + 1024]
                .rearrange("p (j c) -> p j c", c=128),
                in_=wu_int_d[:, :, 128 * i:128 * (i + 1)]
                .rearrange("j p c -> p j c")).then_inc(sem, 16)
            if i >= 1:
                sy.wait_ge(S["s_dq"], i)
            sy.dma_start(out=wdstg[:, 0:1024],
                         in_=wdT[128 * i:128 * (i + 1), :]).then_inc(S["s_wd"], 16)

    def wq_chain(eng, stage, invtile, dstslot, tmp, itmp):
        for j in range(DT):
            eng.drain() if j == 0 else None
            eng.tensor_mul(out=tmp[:, 128 * j:128 * (j + 1)],
                           in0=stage[:, 128 * j:128 * (j + 1)], in1=invtile)
        eng.drain()
        eng.tensor_scalar(out=tmp[:, :], in0=tmp[:, :], scalar1=128.0,
                          scalar2=MAGIC, op0=ALU.add, op1=ALU.add)
        eng.drain()
        eng.tensor_scalar(out=itmp[:, :], in0=tmp[:, :], scalar1=MAGIC,
                          scalar2=H499, op0=ALU.subtract, op1=ALU.add)
        eng.drain()
        return eng.tensor_scalar(out=dstslot, in0=itmp[:, :], scalar1=128,
                                 scalar2=None, op0=ALU.subtract)

    g_pq1.__exit__(None, None, None)
    g_pq0.__exit__(None, None, None)
    g_pb.__exit__(None, None, None)
    g_pt.__exit__(None, None, None)
    g_pg0, ps_g0 = psu("ps_g0", [128, 1024])
    g_pg1, ps_g1 = psu("ps_g1", [128, 1024])
    g_pu0, ps_u0 = psu("ps_u0", [128, 1024])
    g_pu1, ps_u1 = psu("ps_u1", [128, 1024])
    ps_gg = [ps_g0, ps_g1]
    ps_uu = [ps_u0, ps_u1]

    @block.vector
    def _(v):
        wd_i = [0]

        def wd_step():
            if wd_i[0] >= IT:
                return
            i = wd_i[0]
            v.wait_ge(S["s_wd"], 16 * (i + 1))
            if i >= 1:
                v.wait_ge(S["s_wds"], 16 * i)
            v.wait_ge(S["s_ivd"], 48)
            for c in range(2):
                sl = slice(512 * c, 512 * (c + 1))
                v.drain()
                v.tensor_mul(out=b1t[:, sl], in0=wdstg[:, sl], in1=inv_d[:, sl])
            v.drain()
            v.tensor_scalar(out=b1t[:, :], in0=b1t[:, :], scalar1=128.0,
                            scalar2=MAGIC, op0=ALU.add, op1=ALU.add)
            v.drain()
            v.tensor_scalar(out=b1i[:, :], in0=b1t[:, :], scalar1=MAGIC,
                            scalar2=H499, op0=ALU.subtract, op1=ALU.add)
            v.drain()
            v.tensor_scalar(out=wibf[:, 0:1024], in0=b1i[:, :], scalar1=128,
                            scalar2=None, op0=ALU.subtract).then_inc(S["s_dq"], 1)
            wd_i[0] += 1

        def gate_chain(k):
            j, hf = k // 2, k % 2
            v.wait_ge(S["s_sgd"], 16 * (k + 1))
            v.wait_ge(S["s_ire"], IV_IRE_TOT)
            v.wait_ge(S["s_ivd"], 48)
            if k >= 2:
                v.wait_ge(S["s_wsg"], 16 * (k // 2))
            slq = slice(1408 * hf, 1408 * (hf + 1))
            for c in range(3):
                w = min(512, 1408 - 512 * c)
                sl = slice(512 * c, 512 * c + w)
                v.drain()
                v.tensor_mul(out=b1t[:, 0:w], in0=sgst[:, sl],
                             in1=inv_g[:, 1408 * hf + 512 * c:
                                       1408 * hf + 512 * c + w])
                v.drain()
                v.tensor_scalar(out=b1t[:, 0:w], in0=b1t[:, 0:w], scalar1=128.0,
                                scalar2=MAGIC, op0=ALU.add, op1=ALU.add)
                v.drain()
                v.tensor_scalar(out=b1i[:, 0:w], in0=b1t[:, 0:w], scalar1=MAGIC,
                                scalar2=H499, op0=ALU.subtract, op1=ALU.add)
                v.drain()
                op = v.tensor_scalar(out=gqsp[:, 1408 * hf + sl.start:
                                              1408 * hf + sl.start + w],
                                     in0=b1i[:, 0:w], scalar1=128, scalar2=None,
                                     op0=ALU.subtract)
                if c == 2:
                    op.then_inc(S["s_sgq"], 1)

        for k in range(2 * DT):
            gate_chain(k)
        for k in range(2 * IT):
            i = k // 2
            v.wait_ge(S["s_gmm"], k + 1)
            v.wait_ge(S["s_gsl"], k + 1)
            if k >= 2:
                v.wait_ge(S["s_hsp"] if k % 2 == 0 else S["s_hsp2"],
                          16 * (k // 2))
            v.drain()
            v.scalar_tensor_tensor(out=hb[:, 1024 * (k % 2):1024 * (k % 2) + 1024],
                                   in0=ps_uu[k % 2][:, :],
                                   scalar=scal[:, C_RU + i:C_RU + i + 1],
                                   in1=g1b[:, :], op0=ALU.mult,
                                   op1=ALU.mult).then_inc(S["s_gev"], 1)
            v.drain()
            if k == 0:
                v.reduce_max(out=scal[:, C_HM:C_HM + 1], in_=hb[:, 0:1024],
                             axis=AX.X, apply_absolute_value=True
                             ).then_inc(S["s_gev"], 1)
            else:
                v.reduce_max(out=scal[:, C_HMS:C_HMS + 1],
                             in_=hb[:, 1024 * (k % 2):1024 * (k % 2) + 1024],
                             axis=AX.X, apply_absolute_value=True)
                v.drain()
                v.tensor_max(out=scal[:, C_HM:C_HM + 1], in0=scal[:, C_HM:C_HM + 1],
                             in1=scal[:, C_HMS:C_HMS + 1]).then_inc(S["s_gev"], 1)
            if k % 2 == 1:
                wd_step()
        while wd_i[0] < IT:
            wd_step()

    @block.gpsimd
    def _(gp):
        def up_chain(k):
            j, hf = k // 2, k % 2
            gp.wait_ge(S["s_sud"], 16 * (k + 1))
            gp.wait_ge(S["s_ire"], IV_IRE_TOT)
            if k >= 2:
                gp.wait_ge(S["s_wsu"], 16 * (k // 2))
            else:
                pass
            for c in range(3):
                w = min(512, 1408 - 512 * c)
                sl = slice(512 * c, 512 * c + w)
                gp.drain()
                gp.tensor_mul(out=b2t[:, 0:w], in0=sust[:, sl],
                              in1=inv_u[:, 1408 * hf + 512 * c:
                                        1408 * hf + 512 * c + w])
                gp.drain()
                gp.tensor_scalar(out=b2t[:, 0:w], in0=b2t[:, 0:w], scalar1=128.0,
                                 scalar2=MAGIC, op0=ALU.add, op1=ALU.add)
                gp.drain()
                gp.tensor_scalar(out=b2i[:, 0:w], in0=b2t[:, 0:w], scalar1=MAGIC,
                                 scalar2=H499, op0=ALU.subtract, op1=ALU.add)
                gp.drain()
                op = gp.tensor_scalar(out=uqsp[:, 1408 * hf + sl.start:
                                               1408 * hf + sl.start + w],
                                      in0=b2i[:, 0:w], scalar1=128, scalar2=None,
                                      op0=ALU.subtract)
                if c == 2:
                    op.then_inc(S["s_suq"], 1)

        for j in range(DT):
            up_chain(2 * j)
            up_chain(2 * j + 1)
            gp.wait_ge(S["s_sgq"], 2 * (j + 1))
            gp.dma_start(out=wg_int_d[j], in_=gqsp).then_inc(S["s_wsg"], 16)
            gp.wait_ge(S["s_suq"], 2 * (j + 1))
            gp.dma_start(out=wu_int_d[j], in_=uqsp).then_inc(S["s_wsu"], 16)

        wd_s = [0]

        def wd_spill():
            if wd_s[0] >= IT:
                return
            i = wd_s[0]
            gp.wait_ge(S["s_dq"], i + 1)
            gp.dma_start(out=wd_int_d[i], in_=wibf[:, 0:1024]
                         ).then_inc(S["s_wds"], 16)
            wd_s[0] += 1

        for k in range(2 * IT):
            i, nh = k // 2, k % 2
            gp.wait_ge(S["s_gev"], 2 * k + 1)
            gp.dma_start(out=h_spill[128 * i:128 * (i + 1),
                                     1024 * nh:1024 * (nh + 1)],
                         in_=hb[:, 1024 * (k % 2):1024 * (k % 2) + 1024]
                         ).then_inc(S["s_hsp"] if k % 2 == 0 else S["s_hsp2"], 16)
            if k % 2 == 1:
                wd_spill()
        while wd_s[0] < IT:
            wd_spill()
        gp.wait_ge(S["s_gev"], 2 * 2 * IT)
        gp.wait_ge(S["s_hsp"], 16 * IT)
        gp.wait_ge(S["s_hsp2"], 16 * IT)
        gp.dma_start(out=cc2_in, in_=scal[:, C_HM:C_HM + 1]).then_inc(S["s_cc2"], 16)
        gp.wait_ge(S["s_cc2"], 16)
        gp.collective_compute("AllReduce", ALU.max,
                              replica_groups=[[0, 1, 2, 3, 4, 5, 6, 7]],
                              ins=[cc2_in], outs=[cc2_out]).then_inc(S["s_cc2b"], 1)
        gp.wait_ge(S["s_cc2b"], 1)
        gp.dma_start(out=scal[:, C_AR2:C_AR2 + 1], in_=cc2_out
                     ).then_inc(S["s_cc2"], 16)

    @block.tensor
    def _(tn):
        tn.wait_ge(S["s_ev"], 4 * NG)
        tn.wait_ge(S["s_ire"], IV_IRE_TOT)
        tn.wait_ge(S["s_xq"], 16)
        for i in range(IT):
            tn.wait_ge(S["s_sg"] if i % 2 == 0 else S["s_sg2"],
                       32 * (i // 2 + 1))
            for nh in range(2):
                k = 2 * i + nh
                par = k % 2
                if k >= 2:
                    tn.wait_ge(S["s_gev"], 2 * (k - 1) + 1)
                    tn.wait_ge(S["s_gsl"], k - 1)
                pg, pu = ps_gg[par], ps_uu[par]
                mm = None
                for j in range(DT):
                    stg = sgb[:, 1024 * (i % 2) + 128 * j:
                              1024 * (i % 2) + 128 * (j + 1)]
                    stu = sub[:, 1024 * (i % 2) + 128 * j:
                              1024 * (i % 2) + 128 * (j + 1)]
                    c0 = slice(T * j + 1024 * nh, T * j + 1024 * nh + 512)
                    c1 = slice(T * j + 1024 * nh + 512, T * j + 1024 * (nh + 1))
                    first = (j == 0)
                    last = (j == DT - 1)
                    tn.matmul(pg[:, 0:512], stg, xq[:, c0], start=first, stop=last)
                    tn.matmul(pg[:, 512:1024], stg, xq[:, c1], start=first, stop=last)
                    tn.matmul(pu[:, 0:512], stu, xq[:, c0], start=first, stop=last)
                    mm = tn.matmul(pu[:, 512:1024], stu, xq[:, c1],
                                   start=first, stop=last)
                mm.then_inc(S["s_gmm"], 1)

    @block.scalar
    def _(sc):
        sc.wait_ge(S["s_inv"], 13)
        for k in range(2 * IT):
            i = k // 2
            sc.wait_ge(S["s_gmm"], k + 1)
            if k >= 1:
                sc.wait_ge(S["s_gev"], 2 * (k - 1) + 1)
            sc.activation(out=g1b[:, :], in_=ps_gg[k % 2][:, :], func=AF.Silu,
                          scale=scal[:, C_RG + i:C_RG + i + 1]).then_inc(S["s_gsl"], 1)

    g_pu1.__exit__(None, None, None)
    g_pu0.__exit__(None, None, None)
    g_pt2, ps_t2 = psu("ps_t2", [128, 128])
    g_pb2, ps_b2 = psu("ps_b2", [128, 512])

    hrb = repm[:, 0:2048]
    uq = rep3[:, 0:1024]
    i32q = b1i
    outst = rep1[:, 0:1024]

    @block.sync
    def _(sy):
        sy.wait_ge(S["s_gev"], 2 * 2 * IT)
        for i in range(IT):
            if i >= 1:
                sy.wait_ge(S["s_hqs"], 2 * i)
            sy.wait_ge(S["s_hsp"], 16 * IT)
            sy.wait_ge(S["s_hsp2"], 16 * IT)
            sy.dma_start(out=hrb[:, :],
                         in_=h_spill[128 * i:128 * (i + 1), :]
                         ).then_inc(S["s_hrb"], 16)
        for c in range(4):
            for i in range(IT):
                t = c * IT + i
                if t >= 2:
                    sy.wait_ge(S["s_dmm"], t - 1)
                if c == 0:
                    sy.wait_ge(S["s_hqd"], 16 * (i + 1))
                else:
                    sy.wait_ge(S["s_hqd"], 16 * IT)
                sy.wait_ge(S["s_wds"], 16 * IT)
                sem = S["s_hqm"] if t % 2 == 0 else S["s_hqm2"]
                sy.dma_start(out=hqmv[:, 512 * (t % 2):512 * (t % 2) + 512],
                             in_=hq_d[128 * i:128 * (i + 1),
                                      512 * c:512 * (c + 1)]
                             ).then_inc(sem, 16)
                sy.dma_start(out=wdst[:, 1024 * (t % 2):1024 * (t % 2) + 1024]
                             .rearrange("p (hh c) -> p hh c", c=128),
                             in_=wd_int_d[i].rearrange("p (hh c) -> p hh c", c=128)
                             ).then_inc(sem, 16)

    @block.vector
    def _(v):
        v.wait_ge(S["s_inv2"], 1)
        v.drain()
        v.tensor_copy(out=rows[0:1, K_TR:K_TR + 128],
                      in_=ps_t2[0:1, 0:128]).then_inc(S["s_inv2"], 1)
        v.drain()
        v.reduce_max(out=rows[0:1, K_GMAX2:K_GMAX2 + 1],
                     in_=rows[0:1, K_TR:K_TR + 128],
                     axis=AX.X).then_inc(S["s_inv2"], 1)
        v.wait_ge(S["s_inv2"], 4)
        v.drain()
        v.tensor_mul(out=rows[0:1, K_SX20:K_SX20 + 1],
                     in0=rows[0:1, K_GMAX2:K_GMAX2 + 1],
                     in1=rows[0:1, K_SGC2:K_SGC2 + 1]).then_inc(S["s_inv2"], 1)
        v.drain()
        v.tensor_scalar(out=rows[0:1, K_SX2:K_SX2 + 1],
                        in0=rows[0:1, K_SX20:K_SX20 + 1],
                        scalar1=1.0 / 127.0, scalar2=None,
                        op0=ALU.mult).then_inc(S["s_inv2"], 1)
        v.drain()
        v.reciprocal(out=rows[0:1, K_INV2:K_INV2 + 1],
                     in_=rows[0:1, K_SX2:K_SX2 + 1]).then_inc(S["s_inv2"], 1)
        v.wait_ge(S["s_inv2"], 8)
        v.drain()
        v.tensor_copy(out=scal[:, C_SX2B:C_SX2B + 1],
                      in_=ps_b2[:, 0:1]).then_inc(S["s_inv2"], 1)
        v.wait_ge(S["s_inv2"], 10)
        v.drain()
        v.tensor_copy(out=scal[:, C_IV2B:C_IV2B + 1],
                      in_=ps_b2[:, 0:1]).then_inc(S["s_inv2"], 1)
        v.drain()
        v.tensor_scalar(out=scal[:, C_RD:C_RD + DT], in0=scal[:, C_RD:C_RD + DT],
                        scalar1=scal[:, C_SX2B:C_SX2B + 1], scalar2=1.0 / 127.0,
                        op0=ALU.mult, op1=ALU.mult).then_inc(S["s_inv2"], 1)
        for i in range(IT):
            if i >= 1:
                v.wait_ge(S["s_hqd"], 16 * i)
            for h2 in range(2):
                v.wait_ge(S["s_hqs"], 2 * i + h2 + 1)
                v.drain()
                v.tensor_scalar(out=uq[:, :], in0=uq[:, :], scalar1=CLIP_HI,
                                scalar2=0.0, op0=ALU.min, op1=ALU.max)
                v.drain()
                v.tensor_scalar(out=uq[:, :], in0=uq[:, :], scalar1=MAGIC,
                                scalar2=None, op0=ALU.add)
                v.drain()
                v.tensor_scalar(out=i32q[:, :], in0=uq[:, :], scalar1=MAGIC,
                                scalar2=H499, op0=ALU.subtract, op1=ALU.add)
                v.drain()
                v.tensor_scalar(out=hqo[:, 1024 * h2:1024 * (h2 + 1)],
                                in0=i32q[:, :], scalar1=128, scalar2=None,
                                op0=ALU.subtract).then_inc(S["s_hq"], 1)

    @block.scalar
    def _(sc):
        sc.wait_ge(S["s_inv2"], 3)
        sc.activation(out=rows[0:1, K_SGC2:K_SGC2 + 1],
                      in_=clp[0:1, 1:2],
                      func=AF.Sigmoid).then_inc(S["s_inv2"], 1)
        sc.wait_ge(S["s_dq"], IT)
        sc.wait_ge(S["s_inv2"], 12)
        sc.wait_ge(S["s_gsl"], 2 * IT)
        for i in range(IT):
            sc.wait_ge(S["s_hrb"], 16 * (i + 1))
            for h2 in range(2):
                kk = 2 * i + h2
                if kk >= 1:
                    sc.wait_ge(S["s_hq"], kk)
                sc.activation(out=uq[:, :],
                              in_=hrb[:, 1024 * h2:1024 * (h2 + 1)],
                              func=AF.Identity, scale=scal[:, C_IV2B:C_IV2B + 1],
                              bias=b128[:, 0:1]).then_inc(S["s_hqs"], 1)

    @block.gpsimd
    def _(gp):
        for i in range(IT):
            gp.wait_ge(S["s_hq"], 2 * (i + 1))
            gp.dma_start(out=hq_d[128 * i:128 * (i + 1), :],
                         in_=hqo[:, :]).then_inc(S["s_hqd"], 16)
        for m in range(4 * DT):
            gp.wait_ge(S["s_dev"], m + 1)
            c, h = m // DT, m % DT
            gp.dma_start(out=outT[128 * h:128 * (h + 1),
                                  512 * c:512 * (c + 1)],
                         in_=outst[:, 512 * (m % 2):512 * (m % 2) + 512]
                         ).then_inc(S["s_out"] if m % 2 == 0 else S["s_out2"], 16)
        gp.wait_ge(S["s_out"], 16 * 2 * DT)
        gp.wait_ge(S["s_out2"], 16 * 2 * DT)

    @block.tensor
    def _(tn):
        tn.wait_ge(S["s_gev"], 2 * 2 * IT)
        tn.wait_ge(S["s_gsl"], 2 * IT)
        tn.wait_ge(S["s_cc2"], 32)
        tn.transpose(ps_t2[0:1, 0:128], scal[:, C_AR2:C_AR2 + 1],
                     idn[:, :]).then_inc(S["s_inv2"], 1)
        tn.wait_ge(S["s_inv2"], 7)
        tn.matmul(ps_b2[:, 0:1], on1[0:1, :],
                  rows[0:1, K_SX2:K_SX2 + 1],
                  start=True, stop=True).then_inc(S["s_inv2"], 1)
        tn.wait_ge(S["s_inv2"], 9)
        tn.matmul(ps_b2[:, 0:1], on1[0:1, :],
                  rows[0:1, K_INV2:K_INV2 + 1],
                  start=True, stop=True).then_inc(S["s_inv2"], 1)

    g_pb2.__exit__(None, None, None)
    g_pt2.__exit__(None, None, None)
    g_pg1.__exit__(None, None, None)
    g_pg0.__exit__(None, None, None)
    ps_dn = []
    g_pdn = []
    for h in range(DT):
        g, tns = psu("ps_dn%d" % h, [128, 512])
        ps_dn.append(tns)
        g_pdn.append(g)

    @block.tensor
    def _(tn):
        tn.wait_ge(S["s_inv2"], 12)
        for c in range(4):
            for i in range(IT):
                t = c * IT + i
                tn.wait_ge(S["s_hqm"] if t % 2 == 0 else S["s_hqm2"],
                           32 * (t // 2 + 1))
                if i == 0 and c >= 1:
                    tn.wait_ge(S["s_dev"], 8 * c)
                mm = None
                for h in range(DT):
                    std = wdst[:, 1024 * (t % 2) + 128 * h:
                               1024 * (t % 2) + 128 * (h + 1)]
                    mm = tn.matmul(ps_dn[h][:, :], std,
                                   hqmv[:, 512 * (t % 2):512 * (t % 2) + 512],
                                   start=(i == 0), stop=(i == IT - 1))
                mm.then_inc(S["s_dmm"], 1)

    @block.vector
    def _(v):
        v.drain()
        for c in range(4):
            v.wait_ge(S["s_dmm"], IT * (c + 1))
            for h in range(DT):
                m = c * DT + h
                if m >= 2:
                    v.wait_ge(S["s_out"] if m % 2 == 0 else S["s_out2"],
                              16 * (m // 2))
                v.scalar_tensor_tensor(
                    out=outst[:, 512 * (m % 2):512 * (m % 2) + 512],
                    in0=ps_dn[h][:, :], scalar=scal[:, C_RD + h:C_RD + h + 1],
                    in1=acc[:, T * h + 512 * c:T * h + 512 * (c + 1)],
                    op0=ALU.mult, op1=ALU.add).then_inc(S["s_dev"], 1)

    blkg.__exit__(None, None, None)
    return nc

def kernel(L0_attn_x, L0_attn_gamma, L0_attn_w_qkva, L0_ffn_clip_x,
           L0_ffn_clip_down, L0_ffn_w_gate, L0_ffn_w_up, L0_ffn_w_down):
    x = np.asarray(L0_attn_x, dtype=np.float32)
    gamma = np.asarray(L0_attn_gamma, dtype=np.float32)
    wq = np.asarray(L0_attn_w_qkva, dtype=np.float32)
    cx = np.asarray(L0_ffn_clip_x, dtype=np.float32)
    cd = np.asarray(L0_ffn_clip_down, dtype=np.float32)
    wg = np.asarray(L0_ffn_w_gate, dtype=np.float32)
    wu = np.asarray(L0_ffn_w_up, dtype=np.float32)
    wd = np.asarray(L0_ffn_w_down, dtype=np.float32)

    wqT = np.ascontiguousarray(wq.T)
    wgT = np.ascontiguousarray(wg.T)
    wuT = np.ascontiguousarray(wu.T)
    wdT = np.ascontiguousarray(wd.T)
    wg_nat = np.ascontiguousarray(wg)
    wu_nat = np.ascontiguousarray(wu)
    wd_nat = np.ascontiguousarray(wd)
    gammaT = np.ascontiguousarray(gamma.reshape(DT, 128).T)
    clipv = np.empty((128, 2), np.float32)
    clipv[:, 0] = cx.reshape(-1)[0]
    clipv[:, 1] = cd.reshape(-1)[0]
    identm = np.eye(128, dtype=np.float32)
    onesP = np.ones((128, 1), np.float32)
    ones1 = np.ones((1, 128), np.float32)

    nc = build_nc()
    in_maps = []
    for b in range(8):
        in_maps.append({
            "xT": np.ascontiguousarray(x[b].T),
            "wqT": wqT, "wgT": wgT, "wuT": wuT, "wdT": wdT,
            "wg_nat": wg_nat, "wu_nat": wu_nat, "wd_nat": wd_nat,
            "gammaT": gammaT, "clips": clipv, "ident": identm,
            "onesP": onesP, "ones1": ones1,
        })
    res = run_bass_kernel_spmd(nc, in_maps, core_ids=list(range(8)))
    out = np.stack([np.ascontiguousarray(np.asarray(res.results[b]["outT"]).T)
                    for b in range(8)])
    return out


# version 07
# speedup vs baseline: 1.7641x; 1.7641x over previous
import numpy as np
import concourse.bass as bass
import concourse.mybir as mybir
from concourse.bass_utils import run_bass_kernel_spmd

F32 = mybir.dt.float32
I32 = mybir.dt.int32
BF16 = mybir.dt.bfloat16
AF = mybir.ActivationFunctionType
ALU = mybir.AluOpType
AX = mybir.AxisListType

T = 2048
D = 1024
E = 3072
II = 2816
DT = D // 128
IT = II // 128
MAGIC = float(np.float32(2.0 ** 23))
H499 = 0.4999847412109375
CLIP_HI = 255.49
BMG = 128.0 + MAGIC
BN3 = -(128.0 + MAGIC)
CLIP_HI_M = float(np.float32(MAGIC + CLIP_HI))
EPS_LN = 1e-5

GROUPS = [[8 + 2 * g, 9 + 2 * g] for g in range(4)] + \
         [[16 + 2 * g, 17 + 2 * g] for g in range(4)] + \
         [[0 + 2 * g, 1 + 2 * g] for g in range(4)]
NG = len(GROUPS)

C_RG, C_RU, C_RD = 0, 22, 44
C_XM, C_XMS, C_AR1 = 52, 53, 54
C_HM, C_HMS, C_AR2 = 55, 56, 57
C_SXB, C_SX2B, C_IV2B, C_IVB = 58, 59, 60, 61
C_IG, C_IU, C_ID = 64, 86, 108

S_A, S_B = 0, T
RW = 2 * T + 16
K_TR = 0
K_GMAX, K_SGC, K_SX0, K_SX, K_INV = 2*T+0, 2*T+1, 2*T+2, 2*T+3, 2*T+4
K_GMAX2, K_SGC2, K_SX20, K_SX2, K_INV2 = 2*T+5, 2*T+6, 2*T+7, 2*T+8, 2*T+9


def build_nc():
    nc = bass.Bass("TRN2", target_bir_lowering=False)

    xT = nc.dram_tensor("xT", [D, T], F32, kind="ExternalInput").ap()
    wqT = nc.dram_tensor("wqT", [D, E], F32, kind="ExternalInput").ap()
    wdT = nc.dram_tensor("wdT", [II, D], F32, kind="ExternalInput").ap()
    wg_nat = nc.dram_tensor("wg_nat", [II, D], F32, kind="ExternalInput").ap()
    wu_nat = nc.dram_tensor("wu_nat", [II, D], F32, kind="ExternalInput").ap()
    wd_nat = nc.dram_tensor("wd_nat", [D, II], F32, kind="ExternalInput").ap()
    gammaT = nc.dram_tensor("gammaT", [128, DT], F32, kind="ExternalInput").ap()
    clips = nc.dram_tensor("clips", [128, 2], F32, kind="ExternalInput").ap()
    ident = nc.dram_tensor("ident", [128, 128], F32, kind="ExternalInput").ap()
    onesP = nc.dram_tensor("onesP", [128, 1], F32, kind="ExternalInput").ap()
    ones1 = nc.dram_tensor("ones1", [1, 128], F32, kind="ExternalInput").ap()
    outT = nc.dram_tensor("outT", [D, T], F32, kind="ExternalOutput").ap()

    h_spill = nc.dram_tensor("h_spill", [II, T], F32, kind="Internal").ap()
    hq_d = nc.dram_tensor("hq_d", [II, T], BF16, kind="Internal").ap()
    wd_int_d = nc.dram_tensor("wd_int_d", [IT, 128, D], BF16, kind="Internal").ap()
    cc1_in = nc.dram_tensor("cc1_in", [128, 1], F32, kind="Internal").ap()
    cc1_out = nc.dram_tensor("cc1_out", [128, 1], F32, kind="Internal",
                             addr_space="Shared").ap()
    cc2_in = nc.dram_tensor("cc2_in", [128, 1], F32, kind="Internal").ap()
    cc2_out = nc.dram_tensor("cc2_out", [128, 1], F32, kind="Internal",
                             addr_space="Shared").ap()

    sem_names = [
        "s_cld", "s_x", "s_sq", "s_stm", "s_sev", "s_rows", "s_tr", "s_repe",
        "s_tc", "s_xh", "s_xl", "s_nat", "s_nat2", "s_wst", "s_wq", "s_wq2",
        "s_ah", "s_al", "s_mm", "s_ev", "s_sig", "s_scan", "s_res", "s_xm",
        "s_cc1", "s_cc1b", "s_inv", "s_sqr", "s_cst",
        "s_xa", "s_xb", "s_x1", "s_xc", "s_x2", "s_xq",
        "s_ir", "s_ire", "s_ivd", "s_ivr",
        "s_gnd", "s_gnd2", "s_und", "s_und2", "s_ga", "s_ua", "s_q2", "s_q3",
        "s_str", "s_sgev", "s_umm", "s_gmm", "s_gsl",
        "s_gev", "s_hmx", "s_hsp", "s_hsp2", "s_wd", "s_wts", "s_wd2",
        "s_dq", "s_wds",
        "s_cc2", "s_cc2b", "s_inv2", "s_hrb", "s_hqs", "s_hcl", "s_h2c",
        "s_hq", "s_hqd", "s_hqm", "s_hqm2", "s_dmm", "s_dev", "s_out", "s_out2",
    ]
    S = {n: nc.semaphore(n).__enter__() for n in sem_names}

    def sb(name, shape, dt):
        g = nc.sbuf_tensor(name, shape, dt)
        return g, g.__enter__()

    def psu(name, shape):
        g = nc.psum_tensor(name, shape, F32)
        return g, g.__enter__()

    _, rows = sb("rows", [128, RW], F32)
    _, acc = sb("acc", [128, DT * T], F32)
    _, rep1 = sb("rep1", [128, T], F32)
    _, repm = sb("repm", [128, T], F32)
    _, rep3 = sb("rep3", [128, T], F32)
    _, gam = sb("gam", [128, DT], F32)
    _, idn = sb("idn", [128, 128], F32)
    _, onp = sb("onp", [128, 1], F32)
    _, on1 = sb("on1", [1, 128], F32)
    _, clp = sb("clp", [128, 2], F32)
    _, scal = sb("scal", [128, 128], F32)
    _, cst = sb("cst", [128, 3], F32)

    g_xh, xh = sb("xh", [128, DT * T], BF16)
    g_xl, xl = sb("xl", [128, DT * T], BF16)

    gA = []
    def sbA(name, shape, dt):
        g, t = sb(name, shape, dt)
        gA.append(g)
        return t
    xstage = sbA("xstage", [128, T], F32)
    wqst = sbA("wqst", [128, 2 * 256], F32)
    ahg = sbA("ahg", [128, 2 * DT * 256], BF16)
    alg = sbA("alg", [128, 2 * DT * 256], BF16)
    rot = sbA("rot", [128, T], F32)
    evt = sbA("evt", [128, 1024], F32)

    g_pt, ps_t = psu("ps_t", [128, 128])
    g_pb, ps_b = psu("ps_b", [128, 512])
    g_pst, ps_st = psu("ps_st", [1, 1024])
    g_pst2, ps_st2 = psu("ps_st2", [1, 1024])

    blkg = nc.Block()
    block = blkg.__enter__()

    def stsl(buf, g, j, ei):
        base = (g % 2) * (DT * 256) + j * 256 + ei * 128
        return buf[:, base:base + 128]


    @block.sync
    def _(sy):
        for dst, src in ((gam, gammaT), (idn, ident), (onp, onesP),
                         (on1, ones1), (clp, clips)):
            sy.dma_start(out=dst[:, :], in_=src).then_inc(S["s_cld"], 16)
        for j in range(DT):
            if j >= 1:
                sy.wait_ge(S["s_sq"], 4 * j)
                sy.wait_ge(S["s_stm"], 8 * j)
            sy.dma_start(out=xstage[:, :],
                         in_=xT[128 * j:128 * (j + 1), :]).then_inc(S["s_x"], 16)
        for j in range(DT):
            if j == 0:
                sy.wait_ge(S["s_sq"], 4 * DT)
                sy.wait_ge(S["s_stm"], 8 * DT)
            else:
                sy.wait_ge(S["s_tc"], j)
            sy.dma_start(out=xstage[:, :],
                         in_=xT[128 * j:128 * (j + 1), :]).then_inc(S["s_x"], 16)
        for g in range(NG):
            e0 = GROUPS[g][0] * 128
            for j in range(DT):
                k = g * DT + j
                if k >= 2:
                    sy.wait_ge(S["s_al"], k - 1)
                sl = slice(256 * (k % 2), 256 * (k % 2) + 256)
                sy.dma_start(out=wqst[:, sl],
                             in_=wqT[128 * j:128 * (j + 1), e0:e0 + 256]
                             ).then_inc(S["s_wq"] if k % 2 == 0 else S["s_wq2"], 16)

    @block.scalar
    def _(sc):
        for j in range(DT):
            sc.wait_ge(S["s_x"], 16 * (j + 1))
            for c in range(4):
                idx = 4 * j + c
                if idx >= 2:
                    pj, pc = (idx - 2) // 4, (idx - 2) % 4
                    sc.wait_ge(S["s_stm"], 8 * pj + 4 + pc + 1)
                sc.activation(out=evt[:, 512 * (c % 2):512 * (c % 2) + 512],
                              in_=xstage[:, 512 * c:512 * (c + 1)],
                              func=AF.Square).then_inc(S["s_sq"], 1)

    @block.tensor
    def _(tn):
        tn.wait_ge(S["s_cld"], 80)
        for j in range(DT):
            tn.wait_ge(S["s_x"], 16 * (j + 1))
            for c in range(4):
                idx = 4 * j + c
                if idx >= 2:
                    pj, pc = (idx - 2) // 4, (idx - 2) % 4
                    tn.wait_ge(S["s_sev"], 8 * pj + pc + 1)
                tn.matmul(ps_st[0:1, 512 * (c % 2):512 * (c % 2) + 512],
                          onp[:, 0:1], xstage[:, 512 * c:512 * (c + 1)],
                          start=True, stop=True).then_inc(S["s_stm"], 1)
            for c in range(4):
                tn.wait_ge(S["s_sq"], 4 * j + c + 1)
                idx = 4 * j + c
                if idx >= 2:
                    pj, pc = (idx - 2) // 4, (idx - 2) % 4
                    tn.wait_ge(S["s_sev"], 8 * pj + 4 + pc + 1)
                tn.matmul(ps_st2[0:1, 512 * (c % 2):512 * (c % 2) + 512],
                          onp[:, 0:1], evt[:, 512 * (c % 2):512 * (c % 2) + 512],
                          start=True, stop=True).then_inc(S["s_stm"], 1)

    @block.vector
    def _(v):
        v.memset(cst[:, 0:1], BMG)
        v.memset(cst[:, 1:2], H499)
        v.memset(cst[:, 2:3], BN3).then_inc(S["s_cst"], 1)
        for j in range(DT):
            for c in range(4):
                v.wait_ge(S["s_stm"], 8 * j + c + 1)
                src = ps_st[0:1, 512 * (c % 2):512 * (c % 2) + 512]
                dst = rows[0:1, S_A + 512 * c:S_A + 512 * (c + 1)]
                if j > 0:
                    v.drain()
                op = v.tensor_copy(out=dst, in_=src) if j == 0 else \
                    v.tensor_add(out=dst, in0=dst, in1=src)
                op.then_inc(S["s_sev"], 1)
            for c in range(4):
                v.wait_ge(S["s_stm"], 8 * j + 4 + c + 1)
                src = ps_st2[0:1, 512 * (c % 2):512 * (c % 2) + 512]
                dst = rows[0:1, S_B + 512 * c:S_B + 512 * (c + 1)]
                if j > 0:
                    v.drain()
                op = v.tensor_copy(out=dst, in_=src) if j == 0 else \
                    v.tensor_add(out=dst, in0=dst, in1=src)
                op.then_inc(S["s_sev"], 1)
        rA = rows[0:1, S_A:S_A + T]
        rB = rows[0:1, S_B:S_B + T]
        v.drain()
        v.tensor_scalar(out=rA, in0=rA, scalar1=1.0 / D, scalar2=None,
                        op0=ALU.mult).then_inc(S["s_rows"], 1)
        for c in range(4):
            v.wait_ge(S["s_tr"], c + 1)
            v.tensor_copy(out=repm[:, 512 * c:512 * (c + 1)],
                          in_=ps_b[:, :]).then_inc(S["s_repe"], 1)
        v.drain()
        v.tensor_scalar(out=rB, in0=rB, scalar1=1.0 / D, scalar2=None,
                        op0=ALU.mult).then_inc(S["s_rows"], 1)
        v.drain()
        v.tensor_mul(out=rA, in0=rA, in1=rA).then_inc(S["s_rows"], 1)
        v.drain()
        v.tensor_sub(out=rB, in0=rB, in1=rA).then_inc(S["s_rows"], 1)
        v.drain()
        v.reciprocal(out=rA, in_=rB).then_inc(S["s_rows"], 1)
        for c in range(4):
            v.wait_ge(S["s_tr"], 5 + c)
            v.tensor_copy(out=rep1[:, 512 * c:512 * (c + 1)],
                          in_=ps_b[:, :]).then_inc(S["s_repe"], 1)
        v.drain()
        v.tensor_scalar(out=rB, in0=rB, scalar1=EPS_LN, scalar2=None,
                        op0=ALU.add).then_inc(S["s_rows"], 1)
        v.wait_ge(S["s_tr"], 8)
        v.drain()
        v.reciprocal(out=rA, in_=rB).then_inc(S["s_rows"], 1)
        for c in range(4):
            v.wait_ge(S["s_tr"], 9 + c)
            v.tensor_copy(out=rep3[:, 512 * c:512 * (c + 1)],
                          in_=ps_b[:, :]).then_inc(S["s_repe"], 1)

    @block.scalar
    def _(sc):
        sc.wait_ge(S["s_repe"], 8)
        sc.activation(out=rep1[:, :], in_=rep1[:, :],
                      func=AF.Sqrt).then_inc(S["s_sqr"], 1)
        sc.wait_ge(S["s_repe"], 12)
        sc.activation(out=rep3[:, :], in_=rep3[:, :],
                      func=AF.Sqrt).then_inc(S["s_sqr"], 1)

    g_pst2.__exit__(None, None, None)
    g_pst.__exit__(None, None, None)
    g_pq0, ps_q0 = psu("ps_q0", [128, 1024])
    g_pq1, ps_q1 = psu("ps_q1", [128, 1024])
    ps_qq = [ps_q0, ps_q1]

    @block.tensor
    def _(tn):
        nb = 0
        for src_off, rowwait in ((S_A, 1), (S_A, 5), (S_A, 7)):
            tn.wait_ge(S["s_rows"], rowwait)
            for c in range(4):
                if nb >= 1:
                    tn.wait_ge(S["s_repe"], nb)
                tn.matmul(ps_b[:, :], on1[0:1, :],
                          rows[0:1, src_off + 512 * c:src_off + 512 * (c + 1)],
                          start=True, stop=True).then_inc(S["s_tr"], 1)
                nb += 1

    @block.vector
    def _(v):
        v.wait_ge(S["s_repe"], 8)
        for j in range(DT):
            v.wait_ge(S["s_x"], 16 * (DT + j + 1))
            if j >= 1:
                v.wait_ge(S["s_xh"], j)
            v.drain()
            v.scalar_tensor_tensor(out=rot[:, :], in0=repm[:, :], scalar=-1.0,
                                   in1=xstage[:, :], op0=ALU.mult,
                                   op1=ALU.add).then_inc(S["s_tc"], 1)
            v.wait_ge(S["s_xh"], j + 1)
            v.tensor_sub(out=xl[:, T * j:T * (j + 1)], in0=rot[:, :],
                         in1=xh[:, T * j:T * (j + 1)]).then_inc(S["s_xl"], 1)

    @block.scalar
    def _(sc):
        for j in range(DT):
            sc.wait_ge(S["s_tc"], j + 1)
            sc.copy(out=xh[:, T * j:T * (j + 1)],
                    in_=rot[:, :]).then_inc(S["s_xh"], 1)

    @block.vector
    def _(v):
        v.wait_ge(S["s_xl"], DT)
        v.wait_ge(S["s_sqr"], 2)
        v.wait_ge(S["s_stm"], 8 * DT)
        for j in range(DT):
            for h2 in range(2):
                sl = slice(T * j + 1024 * h2, T * j + 1024 * (h2 + 1))
                v.drain()
                v.tensor_add(out=evt[:, 0:1024], in0=xh[:, sl], in1=xl[:, sl])
                v.drain()
                v.tensor_mul(out=evt[:, 0:1024], in0=evt[:, 0:1024],
                             in1=rep3[:, 1024 * h2:1024 * (h2 + 1)])
                v.drain()
                if j == 0 and h2 == 0:
                    v.reduce_max(out=scal[:, C_XM:C_XM + 1], in_=evt[:, 0:1024],
                                 axis=AX.X, apply_absolute_value=True
                                 ).then_inc(S["s_xm"], 1)
                else:
                    v.reduce_max(out=scal[:, C_XMS:C_XMS + 1], in_=evt[:, 0:1024],
                                 axis=AX.X, apply_absolute_value=True)
                    v.drain()
                    v.tensor_max(out=scal[:, C_XM:C_XM + 1],
                                 in0=scal[:, C_XM:C_XM + 1],
                                 in1=scal[:, C_XMS:C_XMS + 1]).then_inc(S["s_xm"], 1)

    nat = []
    for i in range(IT):
        nat.append((wg_nat, 128 * i, 0, 1024, C_RG + i, False))
    for i in range(IT):
        nat.append((wu_nat, 128 * i, 0, 1024, C_RU + i, False))
    for hh in range(DT):
        for part in range(3):
            w0 = 1024 * part
            wid = 1024 if part < 2 else II - 2048
            nat.append((wd_nat, 128 * hh, w0, wid, C_RD + hh, part > 0))

    @block.gpsimd
    def _(gp):
        gp.wait_ge(S["s_xl"], DT)
        for t, (src, r0, w0, wid, col, foldin) in enumerate(nat):
            if t >= 2:
                gp.wait_ge(S["s_wst"], t - 1)
            buf = xstage[:, 1024 * (t % 2):1024 * (t % 2) + 1024]
            sem = S["s_nat"] if t % 2 == 0 else S["s_nat2"]
            gp.dma_start(out=buf[:, 0:wid],
                         in_=src[r0:r0 + 128, w0:w0 + wid]).then_inc(sem, 16)
        gp.wait_ge(S["s_xm"], 16)
        gp.dma_start(out=cc1_in, in_=scal[:, C_XM:C_XM + 1]).then_inc(S["s_cc1"], 16)
        gp.wait_ge(S["s_cc1"], 16)
        gp.collective_compute("AllReduce", ALU.max,
                              replica_groups=[[0, 1, 2, 3, 4, 5, 6, 7]],
                              ins=[cc1_in], outs=[cc1_out]).then_inc(S["s_cc1b"], 1)
        gp.wait_ge(S["s_cc1b"], 1)
        gp.dma_start(out=scal[:, C_AR1:C_AR1 + 1], in_=cc1_out
                     ).then_inc(S["s_cc1"], 16)

    @block.scalar
    def _(sc):
        def sigmoids(ga):
            for kk in range(4):
                k = (ga - 4) * 4 + kk
                nh = kk // 2
                ei = kk % 2
                b = ga * 4 + nh * 2 + ei
                sc.wait_ge(S["s_ev"], b + 1)
                if k >= 1:
                    sc.wait_ge(S["s_scan"], k)
                sc.activation(out=evt[:, 0:1024],
                              in_=rot[:, 1024 * ei:1024 * (ei + 1)],
                              func=AF.Sigmoid).then_inc(S["s_sig"], 1)

        sc.wait_ge(S["s_cld"], 80)
        for g in range(NG):
            for j in range(DT):
                k = g * DT + j
                sc.wait_ge(S["s_wq"] if k % 2 == 0 else S["s_wq2"],
                           16 * (k // 2 + 1))
                if g >= 2 and j == 0:
                    sc.wait_ge(S["s_mm"], 4 * (g - 1))
                base = (g % 2) * (DT * 256) + j * 256
                sl = slice(256 * (k % 2), 256 * (k % 2) + 256)
                sc.activation(out=ahg[:, base:base + 256], in_=wqst[:, sl],
                              func=AF.Copy, scale=gam[:, j:j + 1]
                              ).then_inc(S["s_ah"], 1)
            if 5 <= g <= 8:
                sigmoids(g - 1)

    @block.vector
    def _(v):
        def prep(g):
            for j in range(DT):
                k = g * DT + j
                v.wait_ge(S["s_ah"], k + 1)
                base = (g % 2) * (DT * 256) + j * 256
                sl = slice(256 * (k % 2), 256 * (k % 2) + 256)
                v.scalar_tensor_tensor(out=alg[:, base:base + 256],
                                       in0=wqst[:, sl],
                                       scalar=gam[:, j:j + 1],
                                       in1=ahg[:, base:base + 256],
                                       op0=ALU.mult,
                                       op1=ALU.subtract).then_inc(S["s_al"], 1)

        def consume(g):
            gkind = "kv" if g < 4 else ("a" if g < 8 else "q")
            v.wait_ge(S["s_sqr"], 1)
            for nh in range(2):
                items = []
                for ei in range(2):
                    b = g * 4 + nh * 2 + ei
                    e = GROUPS[g][ei]
                    pq = ps_qq[b % 2]
                    v.wait_ge(S["s_mm"], b + 1)
                    if gkind == "kv":
                        jj = e - 8
                        dst = acc[:, T * jj + 1024 * nh:T * jj + 1024 * (nh + 1)]
                    else:
                        if gkind == "a":
                            sigidx = (g - 4) * 4 + nh * 2 + ei
                            if sigidx >= 2:
                                v.wait_ge(S["s_sig"], sigidx - 1)
                        elif g == 8 and nh == 0:
                            v.wait_ge(S["s_sig"], 16)
                        dst = rot[:, 1024 * ei:1024 * (ei + 1)]
                    v.tensor_mul(out=dst, in0=pq[:, :],
                                 in1=rep1[:, 1024 * nh:1024 * (nh + 1)]
                                 ).then_inc(S["s_ev"], 1)
                    items.append((b, e, ei))
                if gkind == "a":
                    for (b, e, ei) in items:
                        jj = e - 16
                        sigidx = (g - 4) * 4 + nh * 2 + ei
                        v.wait_ge(S["s_sig"], sigidx + 1)
                        seg = acc[:, T * jj + 1024 * nh:T * jj + 1024 * (nh + 1)]
                        ini = 0.0 if nh == 0 else \
                            acc[:, T * jj + 1024 * nh - 1:T * jj + 1024 * nh]
                        v.drain()
                        v.tensor_tensor_scan(out=seg, data0=evt[:, 0:1024],
                                             data1=seg, initial=ini,
                                             op0=ALU.mult, op1=ALU.add
                                             ).then_inc(S["s_scan"], 1)
                elif gkind == "q":
                    for (b, e, ei) in items:
                        jj = e
                        sl = slice(T * jj + 1024 * nh, T * jj + 1024 * (nh + 1))
                        seg = acc[:, sl]
                        src = rot[:, 1024 * ei:1024 * (ei + 1)]
                        v.drain()
                        v.tensor_mul(out=seg, in0=seg, in1=src)
                        v.drain()
                        v.tensor_add(out=seg, in0=seg, in1=xh[:, sl])
                        v.drain()
                        v.tensor_add(out=seg, in0=seg, in1=xl[:, sl])
                        v.drain()
                        v.tensor_add(out=seg, in0=seg,
                                     in1=repm[:, 1024 * nh:1024 * (nh + 1)]
                                     ).then_inc(S["s_res"], 1)

        nat_t = [0]

        def nat_steps(n):
            while nat_t[0] < len(nat) and n > 0:
                t = nat_t[0]
                (srcw, r0, w0, wid, col, foldin) = nat[t]
                buf = xstage[:, 1024 * (t % 2):1024 * (t % 2) + 1024]
                v.wait_ge(S["s_nat"] if t % 2 == 0 else S["s_nat2"],
                          16 * (t // 2 + 1))
                v.drain()
                if not foldin:
                    v.reduce_max(out=scal[:, col:col + 1], in_=buf[:, 0:wid],
                                 axis=AX.X, apply_absolute_value=True
                                 ).then_inc(S["s_wst"], 1)
                else:
                    v.reduce_max(out=scal[:, C_HMS:C_HMS + 1], in_=buf[:, 0:wid],
                                 axis=AX.X, apply_absolute_value=True)
                    v.drain()
                    v.tensor_max(out=scal[:, col:col + 1],
                                 in0=scal[:, col:col + 1],
                                 in1=scal[:, C_HMS:C_HMS + 1]).then_inc(S["s_wst"], 1)
                nat_t[0] += 1
                n -= 1

        prep(0)
        prep(1)
        for g in range(NG):
            consume(g)
            if g + 2 < NG:
                prep(g + 2)
            nat_steps(6)
        nat_steps(len(nat))


    @block.tensor
    def _(tn):
        tn.wait_ge(S["s_xl"], DT)
        b = 0
        for g in range(NG):
            tn.wait_ge(S["s_al"], DT * (g + 1))
            for nh in range(2):
                for ei in range(2):
                    par = b % 2
                    if b >= 2:
                        tn.wait_ge(S["s_ev"], b - 1)
                    p = ps_qq[par]
                    mm = None
                    for j in range(DT):
                        st_h = stsl(ahg, g, j, ei)
                        st_l = stsl(alg, g, j, ei)
                        c0 = slice(T * j + 1024 * nh, T * j + 1024 * nh + 512)
                        c1 = slice(T * j + 1024 * nh + 512, T * j + 1024 * (nh + 1))
                        first = (j == 0)
                        last = (j == DT - 1)
                        tn.matmul(p[:, 0:512], st_h, xh[:, c0], start=first, stop=False)
                        tn.matmul(p[:, 512:1024], st_h, xh[:, c1], start=first, stop=False)
                        tn.matmul(p[:, 0:512], st_h, xl[:, c0], start=False, stop=False)
                        tn.matmul(p[:, 512:1024], st_h, xl[:, c1], start=False, stop=False)
                        tn.matmul(p[:, 0:512], st_l, xh[:, c0], start=False, stop=last)
                        mm = tn.matmul(p[:, 512:1024], st_l, xh[:, c1],
                                       start=False, stop=last)
                    mm.then_inc(S["s_mm"], 1)
                    b += 1

    @block.tensor
    def _(tn):
        tn.wait_ge(S["s_cc1"], 32)
        tn.transpose(ps_t[0:1, 0:128], scal[:, C_AR1:C_AR1 + 1],
                     idn[:, :]).then_inc(S["s_inv"], 1)
        tn.wait_ge(S["s_inv"], 7)
        tn.wait_ge(S["s_repe"], 12)
        tn.matmul(ps_b[:, 0:1], on1[0:1, :], rows[0:1, K_INV:K_INV + 1],
                  start=True, stop=True).then_inc(S["s_inv"], 1)
        tn.wait_ge(S["s_inv"], 10)
        tn.matmul(ps_b[:, 0:1], on1[0:1, :], rows[0:1, K_SX:K_SX + 1],
                  start=True, stop=True).then_inc(S["s_inv"], 1)

    @block.vector
    def _(v):
        v.wait_ge(S["s_inv"], 1)
        v.drain()
        v.tensor_copy(out=rows[0:1, K_TR:K_TR + 128],
                      in_=ps_t[0:1, 0:128]).then_inc(S["s_inv"], 1)
        v.drain()
        v.reduce_max(out=rows[0:1, K_GMAX:K_GMAX + 1],
                     in_=rows[0:1, K_TR:K_TR + 128],
                     axis=AX.X).then_inc(S["s_inv"], 1)

    @block.scalar
    def _(sc):
        sc.wait_ge(S["s_inv"], 3)
        sc.activation(out=rows[0:1, K_SGC:K_SGC + 1],
                      in_=clp[0:1, 0:1],
                      func=AF.Sigmoid).then_inc(S["s_inv"], 1)

    @block.vector
    def _(v):
        v.wait_ge(S["s_inv"], 4)
        v.drain()
        v.tensor_mul(out=rows[0:1, K_SX0:K_SX0 + 1],
                     in0=rows[0:1, K_GMAX:K_GMAX + 1],
                     in1=rows[0:1, K_SGC:K_SGC + 1]).then_inc(S["s_inv"], 1)
        v.drain()
        v.tensor_scalar(out=rows[0:1, K_SX:K_SX + 1],
                        in0=rows[0:1, K_SX0:K_SX0 + 1],
                        scalar1=1.0 / 127.0, scalar2=None,
                        op0=ALU.mult).then_inc(S["s_inv"], 1)
        v.drain()
        v.reciprocal(out=rows[0:1, K_INV:K_INV + 1],
                     in_=rows[0:1, K_SX:K_SX + 1]).then_inc(S["s_inv"], 1)
        v.wait_ge(S["s_inv"], 8)
        v.drain()
        v.tensor_copy(out=scal[:, C_IVB:C_IVB + 1],
                      in_=ps_b[:, 0:1]).then_inc(S["s_inv"], 1)
        v.drain()
        v.tensor_scalar(out=rep3[:, :], in0=rep3[:, :],
                        scalar1=scal[:, C_IVB:C_IVB + 1], scalar2=None,
                        op0=ALU.mult).then_inc(S["s_inv"], 1)
        v.wait_ge(S["s_inv"], 11)
        v.drain()
        v.tensor_copy(out=scal[:, C_SXB:C_SXB + 1],
                      in_=ps_b[:, 0:1]).then_inc(S["s_inv"], 1)
        v.wait_ge(S["s_wst"], 44)
        v.drain()
        v.reciprocal(out=scal[:, C_IG:C_IG + IT], in_=scal[:, C_RG:C_RG + IT])
        v.drain()
        v.tensor_scalar(out=scal[:, C_IG:C_IG + IT], in0=scal[:, C_IG:C_IG + IT],
                        scalar1=127.0, scalar2=None, op0=ALU.mult)
        v.drain()
        v.reciprocal(out=scal[:, C_IU:C_IU + IT], in_=scal[:, C_RU:C_RU + IT])
        v.drain()
        v.tensor_scalar(out=scal[:, C_IU:C_IU + IT], in0=scal[:, C_IU:C_IU + IT],
                        scalar1=127.0, scalar2=None, op0=ALU.mult)
        v.wait_ge(S["s_wst"], 68)
        v.drain()
        v.reciprocal(out=scal[:, C_ID:C_ID + DT], in_=scal[:, C_RD:C_RD + DT])
        v.drain()
        v.tensor_scalar(out=scal[:, C_ID:C_ID + DT], in0=scal[:, C_ID:C_ID + DT],
                        scalar1=127.0, scalar2=None, op0=ALU.mult)
        v.drain()
        v.tensor_scalar(out=scal[:, C_RG:C_RG + IT], in0=scal[:, C_RG:C_RG + IT],
                        scalar1=scal[:, C_SXB:C_SXB + 1], scalar2=1.0 / 127.0,
                        op0=ALU.mult, op1=ALU.mult).then_inc(S["s_inv"], 1)
        v.drain()
        v.tensor_scalar(out=scal[:, C_RU:C_RU + IT], in0=scal[:, C_RU:C_RU + IT],
                        scalar1=scal[:, C_SXB:C_SXB + 1], scalar2=1.0 / 127.0,
                        op0=ALU.mult, op1=ALU.mult).then_inc(S["s_inv"], 1)

    for g in reversed(gA):
        g.__exit__(None, None, None)
    g_cbf, cbf = sb("cbf", [128, 2048], F32)
    g_cbi, cbi = sb("cbi", [128, 2048], I32)
    g_shl, shl = sb("shl", [8, 128], F32)
    xq = xl
    NQ = 2 * DT

    def xsl(q):
        h2, j = q // DT, q % DT
        return slice(T * j + 1024 * h2, T * j + 1024 * (h2 + 1))

    def cfs(q):
        return cbf[:, 1024 * (q % 2):1024 * (q % 2) + 1024]

    def cis(q):
        return cbi[:, 1024 * (q % 2):1024 * (q % 2) + 1024]

    @block.vector
    def _(v):
        v.wait_ge(S["s_mm"], 4 * NG)
        v.wait_ge(S["s_sig"], 16)
        v.wait_ge(S["s_wst"], 68)
        v.wait_ge(S["s_inv"], 12)

        def clip(q):
            if not (0 <= q < NQ):
                return
            v.wait_ge(S["s_x1"], q + 1)
            v.tensor_scalar(out=cfs(q), in0=cfs(q), scalar1=CLIP_HI_M,
                            scalar2=MAGIC, op0=ALU.min,
                            op1=ALU.max).then_inc(S["s_xc"], 1)

        for q in range(NQ):
            if q >= 2:
                v.wait_ge(S["s_x2"], q - 1)
            v.tensor_add(out=cfs(q), in0=xh[:, xsl(q)],
                         in1=xl[:, xsl(q)]).then_inc(S["s_xa"], 1)
            clip(q - 1)
            v.wait_ge(S["s_xa"], q + 1)
            h2 = q // DT
            v.tensor_mul(out=cfs(q), in0=cfs(q),
                         in1=rep3[:, 1024 * h2:1024 * (h2 + 1)]
                         ).then_inc(S["s_xb"], 1)
        clip(NQ - 1)

    @block.scalar
    def _(sc):
        sc.wait_ge(S["s_cst"], 1)
        sc.wait_ge(S["s_mm"], 4 * NG)
        sc.wait_ge(S["s_res"], 16)
        sc.wait_ge(S["s_scan"], 16)

        def a1(q):
            if not (0 <= q < NQ):
                return
            sc.wait_ge(S["s_xb"], q + 1)
            sc.activation(out=cfs(q), in_=cfs(q), func=AF.Identity,
                          bias=cst[:, 0:1]).then_inc(S["s_x1"], 1)

        def a2(q):
            if not (0 <= q < NQ):
                return
            sc.wait_ge(S["s_xc"], q + 1)
            if q >= 2:
                sc.wait_ge(S["s_xq"], q - 1)
            sc.activation(out=cis(q), in_=cfs(q), func=AF.Identity,
                          bias=cst[:, 1:2]).then_inc(S["s_x2"], 1)

        def a3(q):
            if not (0 <= q < NQ):
                return
            sc.wait_ge(S["s_x2"], q + 1)
            sc.activation(out=xq[:, xsl(q)], in_=cis(q), func=AF.Identity,
                          bias=cst[:, 2:3]).then_inc(S["s_xq"], 1)

        for q in range(NQ + 2):
            a1(q)
            a2(q - 1)
            a3(q - 2)

    invrow_d = nc.dram_tensor("invrow_d", [1, D], F32, kind="Internal").ap()
    inv_d = rows[:, 3072:4096]
    irow = rows[0:1, 0:D]

    @block.tensor
    def _(tn):
        tn.wait_ge(S["s_inv"], 14)
        tn.transpose(ps_t[0:DT, 0:128], scal[:, C_ID:C_ID + DT],
                     idn[:, :]).then_inc(S["s_ir"], 1)
        for c in range(2):
            tn.wait_ge(S["s_ivr"], 16)
            if c >= 1:
                tn.wait_ge(S["s_ire"], 2)
            tn.matmul(ps_b[:, 0:512], on1[0:1, :],
                      irow[0:1, 512 * c:512 * (c + 1)],
                      start=True, stop=True).then_inc(S["s_ir"], 1)

    @block.gpsimd
    def _(gp):
        gp.wait_ge(S["s_ire"], 1)
        gp.dma_start(out=invrow_d[0, :], in_=shl[:, :]).then_inc(S["s_ivd"], 16)

    @block.sync
    def _(sy):
        sy.wait_ge(S["s_ivd"], 16)
        sy.dma_start(out=irow, in_=invrow_d[0, :]).then_inc(S["s_ivr"], 16)

    @block.vector
    def _(v):
        v.wait_ge(S["s_ir"], 1)
        v.drain()
        v.tensor_copy(out=shl[:, :], in_=ps_t[0:DT, 0:128]).then_inc(S["s_ire"], 1)
        for c in range(2):
            v.wait_ge(S["s_ir"], 2 + c)
            v.drain()
            v.tensor_copy(out=inv_d[:, 512 * c:512 * (c + 1)],
                          in_=ps_b[:, 0:512]).then_inc(S["s_ire"], 1)

    g_shl.__exit__(None, None, None)
    g_cbi.__exit__(None, None, None)
    g_cbf.__exit__(None, None, None)

    sgb = xh[:, 0:2048]
    sub = xh[:, 2048:4096]
    wibf = xh[:, 4096:5120]
    hqo = xh[:, 5120:7168]
    wdst = xh[:, 7168:9216]
    hqmv = xh[:, 9216:10240]
    qb = xh[:, 10240:14336]

    g_gnat, gnat = sb("gnat", [128, 2048], F32)
    g_unat, unat = sb("unat", [128, 2048], F32)
    g_qi, qi = sb("qi", [128, 4096], I32)
    g_wdi, wdi = sb("wdi", [128, 1024], I32)
    g_idnb, idnb = sb("idnb", [128, 128], BF16)
    b1t = rep3[:, 0:1024]
    wdstg = rows[:, S_B:S_B + 1024]
    g1b = rep1[:, 0:1024]
    hb = repm[:, 0:2048]

    def gsl_(i):
        return gnat[:, 1024 * (i % 2):1024 * (i % 2) + 1024]

    def usl_(i):
        return unat[:, 1024 * (i % 2):1024 * (i % 2) + 1024]

    def qig(i):
        return qi[:, 2048 * (i % 2):2048 * (i % 2) + 1024]

    def qiu(i):
        return qi[:, 2048 * (i % 2) + 1024:2048 * (i % 2) + 2048]

    def qbs(i):
        return qb[:, 2048 * (i % 2):2048 * (i % 2) + 2048]

    g_pq1.__exit__(None, None, None)
    g_pq0.__exit__(None, None, None)
    g_pb.__exit__(None, None, None)
    g_pt.__exit__(None, None, None)
    g_pg, ps_g = psu("ps_g", [128, 1024])
    g_pu0, ps_u0 = psu("ps_u0", [128, 1024])
    g_pu1, ps_u1 = psu("ps_u1", [128, 1024])
    ps_uu = [ps_u0, ps_u1]
    g_tba = nc.psum_tensor("tba", [128, 1024], BF16)
    tba = g_tba.__enter__()
    g_tbb = nc.psum_tensor("tbb", [128, 1024], BF16)
    tbb = g_tbb.__enter__()

    @block.sync
    def _(sy):
        sy.wait_ge(S["s_xq"], 16)
        sy.wait_ge(S["s_ivd"], 16)
        for i in range(IT):
            pg = S["s_gnd"] if i % 2 == 0 else S["s_gnd2"]
            pu = S["s_und"] if i % 2 == 0 else S["s_und2"]
            if i >= 2:
                sy.wait_ge(S["s_q2"], i - 1)
            sy.dma_start(out=gsl_(i),
                         in_=wg_nat[128 * i:128 * (i + 1), :]).then_inc(pg, 16)
            sy.dma_start(out=usl_(i),
                         in_=wu_nat[128 * i:128 * (i + 1), :]).then_inc(pu, 16)

    @block.scalar
    def _(sc):
        sc.wait_ge(S["s_inv"], 14)

        def act1(i):
            if not (0 <= i < IT):
                return
            sc.wait_ge(S["s_gnd"] if i % 2 == 0 else S["s_gnd2"],
                       16 * (i // 2 + 1))
            sc.activation(out=gsl_(i), in_=gsl_(i), func=AF.Identity,
                          scale=scal[:, C_IG + i:C_IG + i + 1],
                          bias=cst[:, 0:1]).then_inc(S["s_ga"], 1)
            sc.wait_ge(S["s_und"] if i % 2 == 0 else S["s_und2"],
                       16 * (i // 2 + 1))
            sc.activation(out=usl_(i), in_=usl_(i), func=AF.Identity,
                          scale=scal[:, C_IU + i:C_IU + i + 1],
                          bias=cst[:, 0:1]).then_inc(S["s_ua"], 1)

        def act2(i):
            if not (0 <= i < IT):
                return
            sc.wait_ge(S["s_ga"], i + 1)
            if i >= 2:
                sc.wait_ge(S["s_q3"], i - 1)
            sc.activation(out=qig(i), in_=gsl_(i), func=AF.Identity,
                          bias=cst[:, 1:2])
            sc.wait_ge(S["s_ua"], i + 1)
            sc.activation(out=qiu(i), in_=usl_(i), func=AF.Identity,
                          bias=cst[:, 1:2]).then_inc(S["s_q2"], 1)

        def silu(k):
            if not (0 <= k < 2 * IT):
                return
            i = k // 2
            sc.wait_ge(S["s_gmm"], k + 1)
            if k >= 1:
                sc.wait_ge(S["s_gev"], k)
            sc.activation(out=g1b[:, :], in_=ps_g[:, :], func=AF.Silu,
                          scale=scal[:, C_RG + i:C_RG + i + 1]
                          ).then_inc(S["s_gsl"], 1)

        def w2(m):
            if not (0 <= m < IT):
                return
            sc.wait_ge(S["s_wts"], m + 1)
            if m >= 1:
                sc.wait_ge(S["s_dq"], m)
            sc.activation(out=wdi[:, :], in_=b1t[:, :], func=AF.Identity,
                          bias=cst[:, 1:2]).then_inc(S["s_wd2"], 1)

        def w3(m):
            if not (0 <= m < IT):
                return
            sc.wait_ge(S["s_wd2"], m + 1)
            if m >= 1:
                sc.wait_ge(S["s_wds"], 16 * m)
            sc.activation(out=wibf[:, 0:1024], in_=wdi[:, :], func=AF.Identity,
                          bias=cst[:, 2:3]).then_inc(S["s_dq"], 1)

        def act3(i):
            if not (0 <= i < IT):
                return
            sc.wait_ge(S["s_q2"], i + 1)
            if i >= 2:
                sc.wait_ge(S["s_str"], 2 * (i - 1))
            sc.activation(out=qbs(i)[:, 0:1024], in_=qig(i), func=AF.Identity,
                          bias=cst[:, 2:3])
            sc.activation(out=qbs(i)[:, 1024:2048], in_=qiu(i),
                          func=AF.Identity,
                          bias=cst[:, 2:3]).then_inc(S["s_q3"], 1)

        act1(0)
        act2(0)
        act3(0)
        act1(1)
        act2(1)
        for m in range(IT + 2):
            act3(m + 1)
            silu(2 * (m - 1))
            silu(2 * (m - 1) + 1)
            w3(m - 2)
            w2(m - 1)
            act1(m + 2)
            act2(m + 2)

    @block.vector
    def _(v):
        v.wait_ge(S["s_inv"], 14)
        v.wait_ge(S["s_cld"], 80)
        v.tensor_copy(out=idnb[:, :], in_=idn[:, :]).then_inc(S["s_cst"], 1)

        def evac(i):
            if not (0 <= i < IT):
                return
            v.wait_ge(S["s_str"], 2 * i + 1)
            if i >= 2:
                v.wait_ge(S["s_gmm"], 2 * (i - 1))
            v.tensor_copy(out=sgb[:, 1024 * (i % 2):1024 * (i % 2) + 1024],
                          in_=tba[:, :]).then_inc(S["s_sgev"], 1)
            v.wait_ge(S["s_str"], 2 * i + 2)
            v.tensor_copy(out=sub[:, 1024 * (i % 2):1024 * (i % 2) + 1024],
                          in_=tbb[:, :]).then_inc(S["s_sgev"], 1)

        def stt(k):
            if not (0 <= k < 2 * IT):
                return
            i = k // 2
            v.wait_ge(S["s_umm"], k + 1)
            v.wait_ge(S["s_gsl"], k + 1)
            if k >= 2:
                v.wait_ge(S["s_hsp"] if k % 2 == 0 else S["s_hsp2"],
                          16 * (k // 2))
            v.scalar_tensor_tensor(out=hb[:, 1024 * (k % 2):1024 * (k % 2) + 1024],
                                   in0=ps_uu[k % 2][:, :],
                                   scalar=scal[:, C_RU + i:C_RU + i + 1],
                                   in1=g1b[:, :], op0=ALU.mult,
                                   op1=ALU.mult).then_inc(S["s_gev"], 1)

        def rmax(i):
            if not (0 <= i < IT):
                return
            v.drain()
            if i == 0:
                v.reduce_max(out=scal[:, C_HM:C_HM + 1], in_=hb[:, :],
                             axis=AX.X, apply_absolute_value=True
                             ).then_inc(S["s_hmx"], 1)
            else:
                v.reduce_max(out=scal[:, C_HMS:C_HMS + 1], in_=hb[:, :],
                             axis=AX.X, apply_absolute_value=True)
                v.drain()
                v.tensor_max(out=scal[:, C_HM:C_HM + 1],
                             in0=scal[:, C_HM:C_HM + 1],
                             in1=scal[:, C_HMS:C_HMS + 1]).then_inc(S["s_hmx"], 1)

        def wdv(m):
            if not (0 <= m < IT):
                return
            v.wait_ge(S["s_wd"], 16 * (m + 1))
            v.wait_ge(S["s_ire"], 3)
            if m >= 1:
                v.wait_ge(S["s_wd2"], m)
            v.drain()
            v.tensor_mul(out=b1t[:, :], in0=wdstg[:, 0:1024], in1=inv_d[:, :])
            v.drain()
            v.tensor_scalar(out=b1t[:, :], in0=b1t[:, :], scalar1=BMG,
                            scalar2=None, op0=ALU.add).then_inc(S["s_wts"], 1)

        evac(0)
        for m in range(IT + 2):
            evac(m + 1)
            stt(2 * (m - 1))
            stt(2 * (m - 1) + 1)
            rmax(m - 1)
            wdv(m)

    @block.gpsimd
    def _(gp):
        def hspill(k):
            if not (0 <= k < 2 * IT):
                return
            gp.wait_ge(S["s_gev"], k + 1)
            gp.dma_start(out=h_spill[128 * (k // 2):128 * (k // 2) + 128,
                                     1024 * (k % 2):1024 * (k % 2) + 1024],
                         in_=hb[:, 1024 * (k % 2):1024 * (k % 2) + 1024]
                         ).then_inc(S["s_hsp"] if k % 2 == 0 else S["s_hsp2"], 16)

        def wspill(m):
            if not (0 <= m < IT):
                return
            gp.wait_ge(S["s_dq"], m + 1)
            gp.dma_start(out=wd_int_d[m], in_=wibf[:, 0:1024]
                         ).then_inc(S["s_wds"], 16)

        def wdstage(m):
            if not (0 <= m < IT):
                return
            if m >= 1:
                gp.wait_ge(S["s_wts"], m)
            gp.dma_start(out=wdstg[:, 0:1024],
                         in_=wdT[128 * m:128 * (m + 1), :]
                         ).then_inc(S["s_wd"], 16)

        for m in range(IT):
            wdstage(m)
            hspill(2 * (m - 2))
            hspill(2 * (m - 2) + 1)
            wspill(m - 2)
        for k in range(2 * IT - 4, 2 * IT):
            hspill(k)
        wspill(IT - 2)
        wspill(IT - 1)
        gp.wait_ge(S["s_hmx"], IT)
        gp.dma_start(out=cc2_in, in_=scal[:, C_HM:C_HM + 1]).then_inc(S["s_cc2"], 16)
        gp.wait_ge(S["s_cc2"], 16)
        gp.collective_compute("AllReduce", ALU.max,
                              replica_groups=[[0, 1, 2, 3, 4, 5, 6, 7]],
                              ins=[cc2_in], outs=[cc2_out]).then_inc(S["s_cc2b"], 1)
        gp.wait_ge(S["s_cc2b"], 1)
        gp.dma_start(out=scal[:, C_AR2:C_AR2 + 1], in_=cc2_out
                     ).then_inc(S["s_cc2"], 16)

    @block.tensor
    def _(tn):
        tn.wait_ge(S["s_ire"], 3)
        tn.wait_ge(S["s_ev"], 4 * NG)
        tn.wait_ge(S["s_cst"], 2)

        def trans(i):
            if not (0 <= i < IT):
                return
            tn.wait_ge(S["s_q3"], i + 1)
            if i >= 1:
                tn.wait_ge(S["s_sgev"], 2 * (i - 1) + 1)
            mm = None
            for j in range(DT):
                mm = tn.transpose(tba[:, 128 * j:128 * (j + 1)],
                                  qb[:, 2048 * (i % 2) + 128 * j:
                                     2048 * (i % 2) + 128 * (j + 1)],
                                  idnb[:, :])
            mm.then_inc(S["s_str"], 1)
            if i >= 1:
                tn.wait_ge(S["s_sgev"], 2 * (i - 1) + 2)
            for j in range(DT):
                mm = tn.transpose(tbb[:, 128 * j:128 * (j + 1)],
                                  qb[:, 2048 * (i % 2) + 1024 + 128 * j:
                                     2048 * (i % 2) + 1024 + 128 * (j + 1)],
                                  idnb[:, :])
            mm.then_inc(S["s_str"], 1)

        def sweep(i):
            tn.wait_ge(S["s_sgev"], 2 * (i + 1))
            for nh in range(2):
                k = 2 * i + nh
                if k >= 2:
                    tn.wait_ge(S["s_gev"], k - 1)
                mm = None
                for j in range(DT):
                    if i <= 1:
                        tn.wait_ge(S["s_xq"], 8 * nh + j + 1)
                    stu = sub[:, 1024 * (i % 2) + 128 * j:
                              1024 * (i % 2) + 128 * (j + 1)]
                    c0 = slice(T * j + 1024 * nh, T * j + 1024 * nh + 512)
                    c1 = slice(T * j + 1024 * nh + 512, T * j + 1024 * (nh + 1))
                    first = (j == 0)
                    last = (j == DT - 1)
                    pu = ps_uu[k % 2]
                    tn.matmul(pu[:, 0:512], stu, xq[:, c0], start=first, stop=last)
                    mm = tn.matmul(pu[:, 512:1024], stu, xq[:, c1],
                                   start=first, stop=last)
                mm.then_inc(S["s_umm"], 1)
                if k >= 1:
                    tn.wait_ge(S["s_gsl"], k)
                for j in range(DT):
                    stg = sgb[:, 1024 * (i % 2) + 128 * j:
                              1024 * (i % 2) + 128 * (j + 1)]
                    c0 = slice(T * j + 1024 * nh, T * j + 1024 * nh + 512)
                    c1 = slice(T * j + 1024 * nh + 512, T * j + 1024 * (nh + 1))
                    first = (j == 0)
                    last = (j == DT - 1)
                    tn.matmul(ps_g[:, 0:512], stg, xq[:, c0], start=first, stop=last)
                    mm = tn.matmul(ps_g[:, 512:1024], stg, xq[:, c1],
                                   start=first, stop=last)
                mm.then_inc(S["s_gmm"], 1)

        trans(0)
        for i in range(IT):
            trans(i + 1)
            sweep(i)

    g_tbb.__exit__(None, None, None)
    g_tba.__exit__(None, None, None)
    g_pu1.__exit__(None, None, None)
    g_pu0.__exit__(None, None, None)
    g_pg.__exit__(None, None, None)
    g_pt2, ps_t2 = psu("ps_t2", [128, 128])
    g_pb2, ps_b2 = psu("ps_b2", [128, 512])

    hrb = repm[:, 0:2048]
    uq2 = rep3[:, 0:2048]
    outst = rep1[:, 0:1024]

    def uqs(kk):
        return uq2[:, 1024 * (kk % 2):1024 * (kk % 2) + 1024]

    def hqi(kk):
        return qi[:, 2048 * (kk % 2):2048 * (kk % 2) + 1024]

    @block.sync
    def _(sy):
        sy.wait_ge(S["s_gev"], 2 * IT)
        sy.wait_ge(S["s_hmx"], IT)
        for i in range(IT):
            if i >= 1:
                sy.wait_ge(S["s_hqs"], 2 * i)
            sy.wait_ge(S["s_hsp"], 16 * IT)
            sy.wait_ge(S["s_hsp2"], 16 * IT)
            sy.dma_start(out=hrb[:, :],
                         in_=h_spill[128 * i:128 * (i + 1), :]
                         ).then_inc(S["s_hrb"], 16)
        for c in range(4):
            for i in range(IT):
                t = c * IT + i
                if t >= 2:
                    sy.wait_ge(S["s_dmm"], t - 1)
                if c == 0:
                    sy.wait_ge(S["s_hqd"], 16 * (i + 1))
                else:
                    sy.wait_ge(S["s_hqd"], 16 * IT)
                sy.wait_ge(S["s_wds"], 16 * IT)
                sem = S["s_hqm"] if t % 2 == 0 else S["s_hqm2"]
                sy.dma_start(out=hqmv[:, 512 * (t % 2):512 * (t % 2) + 512],
                             in_=hq_d[128 * i:128 * (i + 1),
                                      512 * c:512 * (c + 1)]
                             ).then_inc(sem, 16)
                sy.dma_start(out=wdst[:, 1024 * (t % 2):1024 * (t % 2) + 1024]
                             .rearrange("p (hh c) -> p hh c", c=128),
                             in_=wd_int_d[i].rearrange("p (hh c) -> p hh c", c=128)
                             ).then_inc(sem, 16)

    @block.vector
    def _(v):
        v.wait_ge(S["s_inv2"], 1)
        v.drain()
        v.tensor_copy(out=rows[0:1, K_TR:K_TR + 128],
                      in_=ps_t2[0:1, 0:128]).then_inc(S["s_inv2"], 1)
        v.drain()
        v.reduce_max(out=rows[0:1, K_GMAX2:K_GMAX2 + 1],
                     in_=rows[0:1, K_TR:K_TR + 128],
                     axis=AX.X).then_inc(S["s_inv2"], 1)
        v.wait_ge(S["s_inv2"], 4)
        v.drain()
        v.tensor_mul(out=rows[0:1, K_SX20:K_SX20 + 1],
                     in0=rows[0:1, K_GMAX2:K_GMAX2 + 1],
                     in1=rows[0:1, K_SGC2:K_SGC2 + 1]).then_inc(S["s_inv2"], 1)
        v.drain()
        v.tensor_scalar(out=rows[0:1, K_SX2:K_SX2 + 1],
                        in0=rows[0:1, K_SX20:K_SX20 + 1],
                        scalar1=1.0 / 127.0, scalar2=None,
                        op0=ALU.mult).then_inc(S["s_inv2"], 1)
        v.drain()
        v.reciprocal(out=rows[0:1, K_INV2:K_INV2 + 1],
                     in_=rows[0:1, K_SX2:K_SX2 + 1]).then_inc(S["s_inv2"], 1)
        v.wait_ge(S["s_inv2"], 8)
        v.drain()
        v.tensor_copy(out=scal[:, C_SX2B:C_SX2B + 1],
                      in_=ps_b2[:, 0:1]).then_inc(S["s_inv2"], 1)
        v.wait_ge(S["s_inv2"], 10)
        v.drain()
        v.tensor_copy(out=scal[:, C_IV2B:C_IV2B + 1],
                      in_=ps_b2[:, 0:1]).then_inc(S["s_inv2"], 1)
        v.drain()
        v.tensor_scalar(out=scal[:, C_RD:C_RD + DT], in0=scal[:, C_RD:C_RD + DT],
                        scalar1=scal[:, C_SX2B:C_SX2B + 1], scalar2=1.0 / 127.0,
                        op0=ALU.mult, op1=ALU.mult).then_inc(S["s_inv2"], 1)
        for kk in range(2 * IT):
            v.wait_ge(S["s_hqs"], kk + 1)
            v.tensor_scalar(out=uqs(kk), in0=uqs(kk), scalar1=CLIP_HI_M,
                            scalar2=MAGIC, op0=ALU.min,
                            op1=ALU.max).then_inc(S["s_hcl"], 1)

    @block.scalar
    def _(sc):
        sc.wait_ge(S["s_inv2"], 3)
        sc.activation(out=rows[0:1, K_SGC2:K_SGC2 + 1],
                      in_=clp[0:1, 1:2],
                      func=AF.Sigmoid).then_inc(S["s_inv2"], 1)
        sc.wait_ge(S["s_dq"], IT)
        sc.wait_ge(S["s_wts"], IT)
        sc.wait_ge(S["s_q3"], IT)
        sc.wait_ge(S["s_inv2"], 12)
        sc.wait_ge(S["s_gsl"], 2 * IT)

        def ha1(kk):
            if not (0 <= kk < 2 * IT):
                return
            i, h2 = kk // 2, kk % 2
            sc.wait_ge(S["s_hrb"], 16 * (i + 1))
            if kk >= 2:
                sc.wait_ge(S["s_h2c"], kk - 1)
            sc.activation(out=uqs(kk),
                          in_=hrb[:, 1024 * h2:1024 * (h2 + 1)],
                          func=AF.Identity, scale=scal[:, C_IV2B:C_IV2B + 1],
                          bias=cst[:, 0:1]).then_inc(S["s_hqs"], 1)

        def ha2(kk):
            if not (0 <= kk < 2 * IT):
                return
            sc.wait_ge(S["s_hcl"], kk + 1)
            if kk >= 2:
                sc.wait_ge(S["s_hq"], kk - 1)
            sc.activation(out=hqi(kk), in_=uqs(kk), func=AF.Identity,
                          bias=cst[:, 1:2]).then_inc(S["s_h2c"], 1)

        def ha3(kk):
            if not (0 <= kk < 2 * IT):
                return
            i, h2 = kk // 2, kk % 2
            sc.wait_ge(S["s_h2c"], kk + 1)
            if i >= 1:
                sc.wait_ge(S["s_hqd"], 16 * i)
            sc.activation(out=hqo[:, 1024 * h2:1024 * (h2 + 1)],
                          in_=hqi(kk), func=AF.Identity,
                          bias=cst[:, 2:3]).then_inc(S["s_hq"], 1)

        for kk in range(2 * IT + 2):
            ha1(kk)
            ha2(kk - 1)
            ha3(kk - 2)

    @block.gpsimd
    def _(gp):
        for i in range(IT):
            gp.wait_ge(S["s_hq"], 2 * (i + 1))
            gp.dma_start(out=hq_d[128 * i:128 * (i + 1), :],
                         in_=hqo[:, :]).then_inc(S["s_hqd"], 16)
        for m in range(4 * DT):
            gp.wait_ge(S["s_dev"], m + 1)
            c, h = m // DT, m % DT
            gp.dma_start(out=outT[128 * h:128 * (h + 1),
                                  512 * c:512 * (c + 1)],
                         in_=outst[:, 512 * (m % 2):512 * (m % 2) + 512]
                         ).then_inc(S["s_out"] if m % 2 == 0 else S["s_out2"], 16)
        gp.wait_ge(S["s_out"], 16 * 2 * DT)
        gp.wait_ge(S["s_out2"], 16 * 2 * DT)

    @block.tensor
    def _(tn):
        tn.wait_ge(S["s_gev"], 2 * IT)
        tn.wait_ge(S["s_gsl"], 2 * IT)
        tn.wait_ge(S["s_sgev"], 2 * IT)
        tn.wait_ge(S["s_cc2"], 32)
        tn.transpose(ps_t2[0:1, 0:128], scal[:, C_AR2:C_AR2 + 1],
                     idn[:, :]).then_inc(S["s_inv2"], 1)
        tn.wait_ge(S["s_inv2"], 7)
        tn.matmul(ps_b2[:, 0:1], on1[0:1, :],
                  rows[0:1, K_SX2:K_SX2 + 1],
                  start=True, stop=True).then_inc(S["s_inv2"], 1)
        tn.wait_ge(S["s_inv2"], 9)
        tn.matmul(ps_b2[:, 0:1], on1[0:1, :],
                  rows[0:1, K_INV2:K_INV2 + 1],
                  start=True, stop=True).then_inc(S["s_inv2"], 1)

    g_pb2.__exit__(None, None, None)
    g_pt2.__exit__(None, None, None)
    ps_dn = []
    g_pdn = []
    for h in range(DT):
        g, tns = psu("ps_dn%d" % h, [128, 512])
        ps_dn.append(tns)
        g_pdn.append(g)

    @block.tensor
    def _(tn):
        tn.wait_ge(S["s_inv2"], 12)
        for c in range(4):
            for i in range(IT):
                t = c * IT + i
                tn.wait_ge(S["s_hqm"] if t % 2 == 0 else S["s_hqm2"],
                           32 * (t // 2 + 1))
                if i == 0 and c >= 1:
                    tn.wait_ge(S["s_dev"], 8 * c)
                mm = None
                for h in range(DT):
                    std = wdst[:, 1024 * (t % 2) + 128 * h:
                               1024 * (t % 2) + 128 * (h + 1)]
                    mm = tn.matmul(ps_dn[h][:, :], std,
                                   hqmv[:, 512 * (t % 2):512 * (t % 2) + 512],
                                   start=(i == 0), stop=(i == IT - 1))
                mm.then_inc(S["s_dmm"], 1)

    @block.vector
    def _(v):
        v.drain()
        for c in range(4):
            v.wait_ge(S["s_dmm"], IT * (c + 1))
            for h in range(DT):
                m = c * DT + h
                if m >= 2:
                    v.wait_ge(S["s_out"] if m % 2 == 0 else S["s_out2"],
                              16 * (m // 2))
                v.scalar_tensor_tensor(
                    out=outst[:, 512 * (m % 2):512 * (m % 2) + 512],
                    in0=ps_dn[h][:, :], scalar=scal[:, C_RD + h:C_RD + h + 1],
                    in1=acc[:, T * h + 512 * c:T * h + 512 * (c + 1)],
                    op0=ALU.mult, op1=ALU.add).then_inc(S["s_dev"], 1)

    blkg.__exit__(None, None, None)
    return nc

def kernel(L0_attn_x, L0_attn_gamma, L0_attn_w_qkva, L0_ffn_clip_x,
           L0_ffn_clip_down, L0_ffn_w_gate, L0_ffn_w_up, L0_ffn_w_down):
    x = np.asarray(L0_attn_x, dtype=np.float32)
    gamma = np.asarray(L0_attn_gamma, dtype=np.float32)
    wq = np.asarray(L0_attn_w_qkva, dtype=np.float32)
    cx = np.asarray(L0_ffn_clip_x, dtype=np.float32)
    cd = np.asarray(L0_ffn_clip_down, dtype=np.float32)
    wg = np.asarray(L0_ffn_w_gate, dtype=np.float32)
    wu = np.asarray(L0_ffn_w_up, dtype=np.float32)
    wd = np.asarray(L0_ffn_w_down, dtype=np.float32)

    wqT = np.ascontiguousarray(wq.T)
    wdT = np.ascontiguousarray(wd.T)
    wg_nat = np.ascontiguousarray(wg)
    wu_nat = np.ascontiguousarray(wu)
    wd_nat = np.ascontiguousarray(wd)
    gammaT = np.ascontiguousarray(gamma.reshape(DT, 128).T)
    clipv = np.empty((128, 2), np.float32)
    clipv[:, 0] = cx.reshape(-1)[0]
    clipv[:, 1] = cd.reshape(-1)[0]
    identm = np.eye(128, dtype=np.float32)
    onesP = np.ones((128, 1), np.float32)
    ones1 = np.ones((1, 128), np.float32)

    nc = build_nc()
    in_maps = []
    for b in range(8):
        in_maps.append({
            "xT": np.ascontiguousarray(x[b].T),
            "wqT": wqT, "wdT": wdT,
            "wg_nat": wg_nat, "wu_nat": wu_nat, "wd_nat": wd_nat,
            "gammaT": gammaT, "clips": clipv, "ident": identm,
            "onesP": onesP, "ones1": ones1,
        })
    res = run_bass_kernel_spmd(nc, in_maps, core_ids=list(range(8)))
    out = np.stack([np.ascontiguousarray(np.asarray(res.results[b]["outT"]).T)
                    for b in range(8)])
    return out
